# version 04
# baseline (speedup 1.0000x reference)
import contextlib
import numpy as np
import ml_dtypes
import concourse.bass as bass
import concourse.mybir as mybir
from concourse.bass_utils import run_bass_kernel_spmd

F32 = mybir.dt.float32
BF16 = mybir.dt.bfloat16
AF = mybir.ActivationFunctionType
ALU = mybir.AluOpType
AXX = mybir.AxisListType.X

BIG = 3.0e7
D = 2048
DI = 4096
NH = 64
DH = 64
NST = 128
G = 8
GN = G * NST
CONV = DI + 2 * GN
IP = DI + NH + 2 * GN + DI
L = 4096
TMT = 512
CH = 128
NTILE = L // TMT
NCH = TMT // CH
KC = D // 128
NMT = 81
EXP = 8
FI = 1408
NIT = FI // 128
NEI = EXP * NIT
NIC = DI // 128
NCC = CONV // 128

ENGS = ("tensor", "vector", "scalar", "sync", "gpsimd")


class Sem:
    def __init__(self, handle):
        self.h = handle
        self.n = 0


class Prog:
    def __init__(self):
        self.streams = {e: [] for e in ENGS}

    def op(self, eng, meth, *args, inc=None, **kwargs):
        if inc is not None:
            sem, n = inc
            sem.n += n

            def fn(e, meth=meth, args=args, kwargs=kwargs, sem=sem, n=n):
                getattr(e, meth)(*args, **kwargs).then_inc(sem.h, n)
        else:
            def fn(e, meth=meth, args=args, kwargs=kwargs):
                getattr(e, meth)(*args, **kwargs)
        self.streams[eng].append(fn)

    def wait(self, eng, sem, val=None):
        v = sem.n if val is None else val
        if v <= 0:
            return
        self.streams[eng].append(lambda e, h=sem.h, v=v: e.wait_ge(h, v))

    def snap(self, *sems):
        return [(s, s.n) for s in sems]

    def wait_snap(self, eng, snap):
        for s, v in snap:
            self.wait(eng, s, v)


def build_program(dbg=False):
    nc = bass.Bass("TRN2", target_bir_lowering=False)
    p = Prog()
    DK = "ExternalOutput" if dbg else "Internal"
    heavy = (dbg is True)

    d_x = nc.dram_tensor("x", [L, D], F32, kind="ExternalInput").ap()
    d_ipw = nc.dram_tensor("ipw", [IP, D], F32, kind="ExternalInput").ap()
    d_convw = nc.dram_tensor("convw", [CONV, 1, 4], F32, kind="ExternalInput").ap()
    d_convb = nc.dram_tensor("convb", [CONV], F32, kind="ExternalInput").ap()
    d_convnw = nc.dram_tensor("convnw", [CONV], F32, kind="ExternalInput").ap()
    d_alog = nc.dram_tensor("alog", [NH], F32, kind="ExternalInput").ap()
    d_dtw = nc.dram_tensor("dtw", [NH, NH], F32, kind="ExternalInput").ap()
    d_dtb = nc.dram_tensor("dtb", [NH], F32, kind="ExternalInput").ap()
    d_normw = nc.dram_tensor("normw", [D], F32, kind="ExternalInput").ap()
    d_onw = nc.dram_tensor("onw", [DI], F32, kind="ExternalInput").ap()
    d_opw = nc.dram_tensor("opw", [D, DI], F32, kind="ExternalInput").ap()
    d_wr = nc.dram_tensor("wr", [D, EXP], F32, kind="ExternalInput").ap()
    d_wg = nc.dram_tensor("wg", [EXP, D, FI], F32, kind="ExternalInput").ap()
    d_wu = nc.dram_tensor("wu", [EXP, D, FI], F32, kind="ExternalInput").ap()
    d_wd = nc.dram_tensor("wd", [EXP, FI, D], F32, kind="ExternalInput").ap()
    d_identb = nc.dram_tensor("identb", [128, 128], BF16, kind="ExternalInput").ap()
    d_identf = nc.dram_tensor("identf", [128, 128], F32, kind="ExternalInput").ap()
    d_negbigi = nc.dram_tensor("negbigi", [128, 128], BF16, kind="ExternalInput").ap()
    d_lt1 = nc.dram_tensor("lt1", [128, 128], BF16, kind="ExternalInput").ap()
    d_maskz = nc.dram_tensor("maskz", [128, 128], BF16, kind="ExternalInput").ap()
    d_sel8 = nc.dram_tensor("sel8", [8, 8 * 128], F32, kind="ExternalInput").ap()
    d_out = nc.dram_tensor("out", [L, D], F32, kind="ExternalOutput").ap()
    if heavy:
        d_dbg = nc.dram_tensor("dbgv", [128, 64], F32, kind="ExternalOutput").ap()
        d_drs = nc.dram_tensor("dbgrs", [NTILE, NCH, 128, 4], F32, kind="ExternalOutput").ap()
        d_dln = nc.dram_tensor("dbgln", [128, 16], F32, kind="ExternalOutput").ap()
        d_dxn = nc.dram_tensor("dbgxn", [NTILE, 128, KC * 512], BF16, kind="ExternalOutput").ap()
        d_dyf = nc.dram_tensor("dbgyf", [NTILE, 128, NIC * 512], BF16, kind="ExternalOutput").ap()
        d_dcv = nc.dram_tensor("dbgcv", [NCH, 128, CONV], BF16, kind="ExternalOutput").ap()
        d_dh = nc.dram_tensor("dbgh", [NCH, 128, DI], F32, kind="ExternalOutput").ap()
        d_dlm = nc.dram_tensor("dbglm", [NCH, 128, 4096], BF16, kind="ExternalOutput").ap()
        d_dhb = nc.dram_tensor("dbghb", [128, NEI * 512], BF16, kind="ExternalOutput").ap()
        d_dcb = nc.dram_tensor("dbgcb", [8, 512], F32, kind="ExternalOutput").ap()
        d_ddk = nc.dram_tensor("dbgdk", [NCH, 128, 192], F32, kind="ExternalOutput").ap()
        d_dyt = nc.dram_tensor("dbgyt", [NCH, 128, DI], F32, kind="ExternalOutput").ap()

    s_wip = nc.dram_tensor("s_wip", [NMT, 128, KC * 128], BF16, kind=DK).ap()
    s_wop = nc.dram_tensor("s_wop", [NIC, 128, D], BF16, kind=DK).ap()
    s_wg = nc.dram_tensor("s_wg", [EXP, NIT, 128, KC * 128], BF16, kind=DK).ap()
    s_wu = nc.dram_tensor("s_wu", [EXP, NIT, 128, KC * 128], BF16, kind=DK).ap()
    s_wd = nc.dram_tensor("s_wd", [EXP, NIT, 128, D], BF16, kind=DK).ap()
    s_conv = nc.dram_tensor("s_conv", [NCC, 128, L], BF16, kind=DK).ap()
    s_gate = nc.dram_tensor("s_gate", [NIC, 128, TMT], BF16, kind=DK).ap()
    s_a2 = nc.dram_tensor("s_a2", [NTILE * NCH, 128, 128], F32, kind=DK).ap()
    s_z = nc.dram_tensor("s_z", [L, D], F32, kind=DK).ap()
    s_x1f = nc.dram_tensor("s_x1f", [L, D], F32, kind=DK).ap()
    s_x1b = nc.dram_tensor("s_x1b", [NTILE * NCH, 128, D], BF16, kind=DK).ap()

    ctx = contextlib.ExitStack()
    E = ctx.enter_context

    sW = Sem(E(nc.semaphore("sW")))
    sG = Sem(E(nc.semaphore("sG")))
    sS = Sem(E(nc.semaphore("sS")))
    sP = Sem(E(nc.semaphore("sP")))
    sV = Sem(E(nc.semaphore("sV")))
    sA = Sem(E(nc.semaphore("sA")))
    ALLSEMS = (sW, sG, sS, sP, sV, sA)

    def barrier(engs=ENGS):
        snap = p.snap(*ALLSEMS)
        for e in engs:
            p.wait_snap(e, snap)

    identb = E(nc.sbuf_tensor("sb_identb", [128, 128], BF16))
    identf = E(nc.sbuf_tensor("sb_identf", [128, 128], F32))
    negbigi = E(nc.sbuf_tensor("sb_negbigi", [128, 128], BF16))
    lt1 = E(nc.sbuf_tensor("sb_lt1", [128, 128], BF16))
    maskz = E(nc.sbuf_tensor("sb_maskz", [128, 128], BF16))
    sel8 = E(nc.sbuf_tensor("sb_sel8", [8, 8 * 128], F32))
    onesf_row = E(nc.sbuf_tensor("onesf_row", [1, 128], F32))
    onesb_col = E(nc.sbuf_tensor("onesb_col", [128, 1], BF16))
    wrb = E(nc.sbuf_tensor("wrb", [128, KC * 8], BF16))
    c128a = E(nc.sbuf_tensor("c128a", [128, 8], F32))
    snapb = E(nc.sbuf_tensor("snapb", [128, 16], F32)) if heavy else None
    spc = E(nc.sbuf_tensor("spc", [128, 1], F32))
    flv = E(nc.sbuf_tensor("flv", [128, 512], F32))
    fla = E(nc.sbuf_tensor("fla", [128, 128], F32))
    mctx = contextlib.ExitStack()
    ME = mctx.enter_context
    convw_c = ME(nc.sbuf_tensor("convw_c", [128, NCC * 4], F32))
    convb_c = ME(nc.sbuf_tensor("convb_c", [128, NCC], F32))
    convnw_c = ME(nc.sbuf_tensor("convnw_c", [128, NCC], F32))
    nw_c = ME(nc.sbuf_tensor("nw_c", [128, KC], F32))
    onw_c = ME(nc.sbuf_tensor("onw_c", [128, NIC], F32))
    neA_col = ME(nc.sbuf_tensor("neA_col", [NH, 1], F32))
    dtb_col = ME(nc.sbuf_tensor("dtb_col", [NH, 1], F32))
    dtwT = ME(nc.sbuf_tensor("dtwT", [NH, NH], F32))
    h_st = ME(nc.sbuf_tensor("h_st", [128, NH * DH], F32))
    hb_st = ME(nc.sbuf_tensor("hb_st", [128, NH * DH], BF16))
    stgP = [ME(nc.sbuf_tensor(f"stgP{i}", [2, 1024], F32)) for i in range(2)]
    stgN = [ME(nc.sbuf_tensor(f"stgN{i}", [2, 1024], F32)) for i in range(2)]
    wipbuf = [ME(nc.sbuf_tensor(f"wipbuf{i}", [128, 2048], BF16)) for i in range(2)]
    cdrow = ME(nc.sbuf_tensor("cdrow", [1, 64], F32))
    CDsb = ME(nc.sbuf_tensor("CDsb", [128, 64], F32))
    expacs = ME(nc.sbuf_tensor("expacs", [128, 64], BF16))
    exprc = ME(nc.sbuf_tensor("exprc", [128, 64], BF16))
    a2 = ME(nc.sbuf_tensor("a2", [128, 128], F32))
    rcfm = ME(nc.sbuf_tensor("rcfm", [64, 128], F32))
    rsr = ME(nc.sbuf_tensor("rsr", [1, 256], F32))
    conv_sb = ME(nc.sbuf_tensor("convsb", [128, NCC * 132], BF16))
    pwraw = ME(nc.sbuf_tensor("pwraw", [128, 512], F32))
    pwstg = ME(nc.sbuf_tensor("pwstg", [128, 512], BF16))

    pmm = [E(nc.psum_tensor(f"pmm{i}", [128, 512], F32)) for i in range(4)]
    pR = E(nc.psum_tensor("pR", [128, 1024], F32))
    pT1 = E(nc.psum_tensor("pT1", [128, 1024], BF16))
    pT2 = E(nc.psum_tensor("pT2", [128, 1024], BF16))

    for dst, src in ((identb, d_identb), (identf, d_identf), (negbigi, d_negbigi),
                     (lt1, d_lt1), (maskz, d_maskz), (sel8, d_sel8)):
        p.op("sync", "dma_start", out=dst[:], in_=src[:], inc=(sW, 16))
    p.op("vector", "memset", onesf_row[:], 1.0)
    p.op("vector", "memset", onesb_col[:], 1.0)
    p.op("vector", "memset", h_st[:], 0.0)
    p.op("vector", "memset", hb_st[:], 0.0)
    for i in range(2):
        p.op("vector", "memset", stgP[i][0:2, :], 1.0)
        p.op("vector", "memset", stgN[i][0:2, :], 1.0, inc=(sV, 1))
    cw = d_convw.squeeze(1).rearrange("(ci pp) j -> pp ci j", pp=128)
    for i in range(8):
        p.op("sync", "dma_start",
             out=convw_c[:, i * 24:(i + 1) * 24].rearrange("pp (ci j) -> pp ci j", j=4),
             in_=cw[:, i * 6:(i + 1) * 6, :], inc=(sW, 16))
    with nc.sbuf_tensor("smallv", [128, 640], F32) as smallv, \
         nc.sbuf_tensor("wr_f", [128, KC * 8], F32) as wr_f:
        p.op("sync", "dma_start", out=smallv[0:48, 0:128],
             in_=d_convb.rearrange("(a b) -> a b", b=128), inc=(sW, 16))
        p.op("sync", "dma_start", out=smallv[0:48, 128:256],
             in_=d_convnw.rearrange("(a b) -> a b", b=128), inc=(sW, 16))
        p.op("sync", "dma_start", out=smallv[0:16, 256:384],
             in_=d_normw.rearrange("(a b) -> a b", b=128), inc=(sW, 16))
        p.op("sync", "dma_start", out=smallv[0:32, 384:512],
             in_=d_onw.rearrange("(a b) -> a b", b=128), inc=(sW, 16))
        p.op("sync", "dma_start", out=smallv[0:1, 512:576],
             in_=d_alog.rearrange("(a b) -> a b", a=1), inc=(sW, 16))
        p.op("sync", "dma_start", out=smallv[1:2, 512:576],
             in_=d_dtb.rearrange("(a b) -> a b", a=1), inc=(sW, 16))
        p.op("sync", "dma_start", out=dtwT[:, :], in_=d_dtw[:, :], inc=(sW, 16))
        p.op("sync", "dma_start",
             out=wr_f[:].rearrange("pp (kc e) -> pp kc e", e=8),
             in_=d_wr.rearrange("(kc pp) e -> pp kc e", pp=128), inc=(sW, 16))
        p.wait("tensor", sW)
        p.op("tensor", "transpose", pR[:, 0:48], smallv[0:48, 0:128], identf[0:48, 0:48])
        p.op("tensor", "transpose", pR[:, 48:96], smallv[0:48, 128:256],
             identf[0:48, 0:48])
        p.op("tensor", "transpose", pR[:, 96:112], smallv[0:16, 256:384],
             identf[0:16, 0:16])
        p.op("tensor", "transpose", pR[:, 112:144], smallv[0:32, 384:512],
             identf[0:32, 0:32])
        p.op("tensor", "transpose", pR[0:64, 144:146], smallv[0:2, 512:576],
             identf[0:2, 0:2])
        p.op("tensor", "transpose", pR[0:64, 160:224], dtwT[0:64, 0:64],
             identf[0:64, 0:64], inc=(sP, 1))
        p.wait("vector", sP)
        p.wait("vector", sW)
        p.op("vector", "tensor_copy", convb_c[:], pR[:, 0:48])
        p.op("vector", "tensor_copy", convnw_c[:], pR[:, 48:96])
        p.op("vector", "tensor_copy", nw_c[:], pR[:, 96:112])
        p.op("vector", "tensor_copy", onw_c[:], pR[:, 112:144])
        p.op("vector", "tensor_copy", dtb_col[:], pR[0:64, 145:146])
        p.op("vector", "tensor_copy", wrb[:], wr_f[:])
        p.op("vector", "tensor_copy", c128a[0:64, 0:1], pR[0:64, 144:145], inc=(sV, 1))
        p.wait("scalar", sV)
        p.op("scalar", "activation", neA_col[:], c128a[0:64, 0:1], AF.Exp, inc=(sA, 1))
        p.wait("vector", sA)
        p.op("vector", "tensor_scalar", neA_col[:], neA_col[:], -1.0, None, ALU.mult)
        p.op("vector", "tensor_copy", dtwT[0:64, 0:64], pR[0:64, 160:224])
        p.op("vector", "memset", wr_f[:, 0:8], 0.0, inc=(sV, 1))
        p.op("vector", "memset",
             conv_sb[:].rearrange("pp (ci w) -> pp ci w", ci=NCC)[:, :, 0:3], 0.0,
             inc=(sV, 1))

        if heavy:
            p.op("gpsimd", "dma_start", out=d_dbg[:, 0:16], in_=nw_c[:], inc=(sS, 16))
            p.op("gpsimd", "dma_start", out=d_dbg[:, 16:48], in_=onw_c[:], inc=(sS, 16))
            p.op("gpsimd", "dma_start", out=d_dbg[:, 48:64],
                 in_=convb_c[:, 0:16], inc=(sS, 16))

    barrier()

    with nc.sbuf_tensor("praw", [128, 2048], F32) as raw, \
         nc.sbuf_tensor("pcast", [128, 2048], BF16) as cast, \
         nc.sbuf_tensor("pstage", [128, 2048], BF16) as stage:

        ld_done = []
        st_done = []
        ev_done = []

        def prep_iter(n_rows, load_in, do_tr, fold_col, store_out):
            it = len(ld_done)
            if it >= 1:
                p.wait("sync", sV, ev_done[it - 1][0])
            lo = raw[0:n_rows, 0:2048]
            if load_in.ndim == 3:
                lo = lo.rearrange("pp (a b) -> pp a b", a=load_in.shape[1])
            p.op("sync", "dma_start", out=lo, in_=load_in, inc=(sW, 16))
            ld_done.append(sW.n)
            p.wait("vector", sW, ld_done[it])
            if it >= 1:
                p.wait("vector", sS, st_done[it - 1])
            if not do_tr:
                p.op("vector", "tensor_copy", stage[0:n_rows, 0:2048],
                     raw[0:n_rows, 0:2048], inc=(sV, 1))
                ev_done.append((sV.n, sV.n))
            else:
                p.op("vector", "tensor_copy", cast[0:n_rows, 0:2048],
                     raw[0:n_rows, 0:2048], inc=(sV, 1))
                raw_free = sV.n
                if n_rows < 128:
                    p.op("vector", "memset", cast[n_rows:128, 0:2048], 0.0, inc=(sV, 1))
                cast_v = sV.n
                p.wait("tensor", sV, cast_v)
                if it >= 1:
                    p.wait("tensor", sV, ev_done[it - 1][1])
                for kc in range(16):
                    dst = pT1 if kc < 8 else pT2
                    p.op("tensor", "transpose", dst[:, (kc % 8) * 128:(kc % 8 + 1) * 128],
                         cast[0:128, kc * 128:(kc + 1) * 128], identb[:],
                         inc=(sP, 1) if kc % 8 == 7 else None)
                p.wait("vector", sP)
                for kc in range(16):
                    src = pT1 if kc < 8 else pT2
                    sl = (kc % 8) * 128
                    if fold_col is not None:
                        p.op("vector", "tensor_scalar", stage[:, kc * 128:(kc + 1) * 128],
                             src[:, sl:sl + 128], fold_col(kc), None, ALU.mult,
                             inc=(sV, 1) if kc == 15 else None)
                    else:
                        p.op("vector", "tensor_copy", stage[:, kc * 128:(kc + 1) * 128],
                             src[:, sl:sl + 128], inc=(sV, 1) if kc == 15 else None)
                ev_done.append((raw_free, sV.n))
            p.wait("gpsimd", sV, ev_done[it][1])
            p.op("gpsimd", "dma_start", out=store_out, in_=stage[:], inc=(sS, 16))
            st_done.append(sS.n)

        def ip_rows(mt):
            if mt < 32:
                return mt * 128, 128
            if mt < 40:
                return DI + NH + (mt - 32) * 128, 128
            if mt < 48:
                return DI + NH + GN + (mt - 40) * 128, 128
            if mt < 80:
                return DI + NH + 2 * GN + (mt - 48) * 128, 128
            return DI, 64

        for mt in range(NMT):
            r0, nr = ip_rows(mt)
            prep_iter(nr, d_ipw[r0:r0 + nr, :], True,
                      lambda kc: nw_c[:, kc:kc + 1], s_wip[mt])
        for ic in range(NIC):
            src = d_opw[:, ic * 128:(ic + 1) * 128].rearrange("(ob pp) i -> pp ob i", pp=128)
            prep_iter(128, src, True, lambda kc, ic=ic: onw_c[:, ic:ic + 1], s_wop[ic])

    barrier()

    ip_load_done = [None] * (NMT * NTILE)
    ip_use_done = [None] * (NMT * NTILE)
    mcarry = {}
    pw_pairs = []
    for e in range(EXP):
        for it_ in range(NIT):
            for hf in range(4):
                rs = slice(hf * 512, (hf + 1) * 512)
                pw_pairs.append((d_wg[e, rs, it_ * 128:(it_ + 1) * 128].rearrange(
                    "(kc pp) i -> pp kc i", pp=128), s_wg[e][it_][:, rs]))
                pw_pairs.append((d_wu[e, rs, it_ * 128:(it_ + 1) * 128].rearrange(
                    "(kc pp) i -> pp kc i", pp=128), s_wu[e][it_][:, rs]))
    for e in range(EXP):
        for ic in range(NIT):
            for hf in range(4):
                rs = slice(hf * 512, (hf + 1) * 512)
                pw_pairs.append((d_wd[e, ic * 128:(ic + 1) * 128, rs],
                                 s_wd[e][ic][:, rs]))
    pw_state = {"k": 0, "cast": None, "st": None}

    def pw_step(n):
        for _ in range(n):
            if not pw_pairs:
                return
            load_in, store_out = pw_pairs.pop(0)
            k = pw_state["k"]
            if k >= 1:
                p.wait("sync", sA, pw_state["cast"])
            lo = pwraw[0:128, 0:512]
            if load_in.ndim == 3:
                lo = lo.rearrange("pp (a b) -> pp a b", a=load_in.shape[1])
            p.op("sync", "dma_start", out=lo, in_=load_in, inc=(sW, 16))
            ldv = sW.n
            p.wait("scalar", sW, ldv)
            if k >= 1:
                p.wait("scalar", sS, pw_state["st"])
            p.op("scalar", "activation", pwstg[:], pwraw[:], AF.Copy, inc=(sA, 1))
            pw_state["cast"] = sA.n
            p.wait("gpsimd", sA, pw_state["cast"])
            p.op("gpsimd", "dma_start", out=store_out, in_=pwstg[:], inc=(sS, 16))
            pw_state["st"] = sS.n
            pw_state["k"] = k + 1

    for t in range(NTILE):
        tile_ctx = contextlib.ExitStack()
        TE = tile_ctx.enter_context
        xn_fm = TE(nc.sbuf_tensor(f"xnfm{t}", [128, KC * 512], BF16))
        Yfm = TE(nc.sbuf_tensor(f"Yfm{t}", [128, NIC * 512], BF16))

        with nc.sbuf_tensor(f"xbuf{t}", [128, 2 * 2048], F32) as xbuf, \
             nc.sbuf_tensor(f"xtmb{t}", [128, 2048], BF16) as xtmb, \
             nc.sbuf_tensor(f"sqj{t}", [128, 2048], BF16) as sqj:
            xld = [None] * NCH
            if t > 0:
                p.wait("gpsimd", sV, mcarry["x1v"])
                p.wait("vector", sP, mcarry["xtr"])
                p.wait("scalar", sV, mcarry["x1v"])

            def load_x(tc):
                p.op("gpsimd", "dma_start",
                     out=xbuf[:, (tc % 2) * 2048:(tc % 2 + 1) * 2048],
                     in_=d_x[t * TMT + tc * 128: t * TMT + (tc + 1) * 128, :],
                     inc=(sG, 16))
                xld[tc] = sG.n

            load_x(0)
            load_x(1)
            for tc in range(NCH):
                xb = xbuf[:, (tc % 2) * 2048:(tc % 2 + 1) * 2048]
                p.wait("scalar", sG, xld[tc])
                if tc >= 1:
                    p.wait("scalar", sV)
                p.op("scalar", "activation", sqj[:], xb, AF.Square,
                     accum_out=c128a[:, 0:1])
                p.op("scalar", "activation", fla[:, 64:128], fla[:, 0:64], AF.Copy)
                p.op("scalar", "activation", fla[:, 64:128], fla[:, 0:64], AF.Copy, inc=(sA, 1))
                p.wait("vector", sA)
                p.op("vector", "tensor_scalar", c128a[:, 4:5], c128a[:, 0:1],
                     1.0 / D, 1e-6, ALU.mult, ALU.add)
                p.op("vector", "tensor_scalar", flv[:, 256:512], flv[:, 0:256], 1.0, None, ALU.mult)
                p.op("vector", "tensor_scalar", flv[:, 256:512], flv[:, 0:256], 1.0, None, ALU.mult, inc=(sV, 1))
                p.wait("scalar", sV)
                p.op("scalar", "activation", c128a[:, 6:7], c128a[:, 4:5], AF.Sqrt)
                p.op("scalar", "activation", fla[:, 64:128], fla[:, 0:64], AF.Copy)
                p.op("scalar", "activation", fla[:, 64:128], fla[:, 0:64], AF.Copy, inc=(sA, 1))
                p.wait("vector", sA)
                p.op("vector", "reciprocal", c128a[:, 7:8], c128a[:, 6:7])
                p.op("vector", "tensor_scalar", flv[:, 256:512], flv[:, 0:256], 1.0, None, ALU.mult)
                p.op("vector", "tensor_scalar", flv[:, 256:512], flv[:, 0:256], 1.0, None, ALU.mult)
                p.op("vector", "tensor_scalar", xtmb[:], xb, c128a[:, 7:8], None,
                     ALU.mult, inc=(sV, 1))

                if tc + 2 < NCH:
                    p.wait("gpsimd", sV)
                    load_x(tc + 2)
                p.wait("tensor", sV)
                for kc in range(16):
                    dst = pT1 if kc < 8 else pT2
                    p.op("tensor", "transpose", dst[:, (kc % 8) * 128:(kc % 8 + 1) * 128],
                         xtmb[:, kc * 128:(kc + 1) * 128], identb[:],
                         inc=(sP, 1) if kc % 8 == 7 else None)
                p.wait("vector", sP)
                for kc in range(16):
                    src = pT1 if kc < 8 else pT2
                    p.op("vector", "tensor_copy",
                         xn_fm[:, kc * 512 + tc * 128: kc * 512 + (tc + 1) * 128],
                         src[:, (kc % 8) * 128:(kc % 8 + 1) * 128],
                         inc=(sV, 1) if kc == 15 else None)
            xn_ready = sV.n
            mcarry["xtr"] = sP.n
        if heavy:
            p.wait("gpsimd", sV, xn_ready)
            p.op("gpsimd", "dma_start", out=d_dxn[t], in_=xn_fm[:], inc=(sS, 16))

        with nc.sbuf_tensor(f"afm{t}", [64, 512], F32) as Afm:
            dtctx = contextlib.ExitStack()
            convstage = dtctx.enter_context(
                nc.sbuf_tensor(f"cst{t}", [128, 2 * 512], BF16))
            sgstage = dtctx.enter_context(
                nc.sbuf_tensor(f"sgs{t}", [128, 2 * 512], BF16))
            Adt = dtctx.enter_context(nc.sbuf_tensor(f"adt{t}", [64, 512], F32))
            dt1 = dtctx.enter_context(nc.sbuf_tensor(f"dt1{t}", [64, 512], F32))
            dt2 = dtctx.enter_context(nc.sbuf_tensor(f"dt2{t}", [64, 512], F32))
            dt3 = dtctx.enter_context(nc.sbuf_tensor(f"dt3{t}", [64, 512], F32))

            evac_ref = [None] * NMT
            cs_store = []
            sg_store = []
            for mt in range(NMT):
                gi = t * NMT + mt
                if gi >= 2 and ip_use_done[gi - 2] is not None:
                    p.wait("sync", sP, ip_use_done[gi - 2])
                p.op("sync", "dma_start", out=wipbuf[gi % 2][:], in_=s_wip[mt],
                     inc=(sW, 16))
                ip_load_done[gi] = sW.n
                p.wait("tensor", sW, ip_load_done[gi])
                if mt == 0:
                    p.wait("tensor", sV, xn_ready)
                if mt >= 4 and evac_ref[mt - 4] is not None:
                    s_, v_ = evac_ref[mt - 4]
                    p.wait("tensor", s_, v_)
                for kc in range(16):
                    p.op("tensor", "matmul", pmm[mt % 4][:],
                         wipbuf[gi % 2][:, kc * 128:(kc + 1) * 128],
                         xn_fm[:, kc * 512:(kc + 1) * 512],
                         start=(kc == 0), stop=(kc == 15),
                         inc=(sP, 1) if kc == 15 else None)
                ip_use_done[gi] = sP.n
                mmv = sP.n
                if mt < 48:
                    p.wait("vector", sP, mmv)
                    if mt >= 2:
                        p.wait("vector", sS, cs_store[mt - 2])
                    p.op("vector", "tensor_copy",
                         convstage[:, (mt % 2) * 512:(mt % 2 + 1) * 512],
                         pmm[mt % 4][:], inc=(sV, 1))
                    evac_ref[mt] = (sV, sV.n)
                    p.wait("gpsimd", sV)
                    p.op("gpsimd", "dma_start",
                         out=s_conv[mt, :, t * TMT: t * TMT + 512],
                         in_=convstage[:, (mt % 2) * 512:(mt % 2 + 1) * 512],
                         inc=(sS, 16))
                    cs_store.append(sS.n)
                elif mt < 80:
                    g_i = mt - 48
                    p.wait("scalar", sP, mmv)
                    if g_i >= 2:
                        p.wait("scalar", sS, sg_store[g_i - 2])
                    p.op("scalar", "activation",
                         sgstage[:, (g_i % 2) * 512:(g_i % 2 + 1) * 512],
                         pmm[mt % 4][:], AF.Silu, inc=(sA, 1))
                    evac_ref[mt] = (sA, sA.n)
                    p.wait("gpsimd", sA)
                    p.op("gpsimd", "dma_start", out=s_gate[g_i],
                         in_=sgstage[:, (g_i % 2) * 512:(g_i % 2 + 1) * 512],
                         inc=(sS, 16))
                    sg_store.append(sS.n)
                else:
                    p.wait("vector", sP, mmv)
                    p.op("vector", "tensor_copy", Adt[:], pmm[mt % 4][0:64, :])
                    p.op("vector", "tensor_scalar", flv[:, 256:512], flv[:, 0:256], 1.0, None, ALU.mult)
                    p.op("vector", "tensor_scalar", flv[:, 256:512], flv[:, 0:256], 1.0, None, ALU.mult, inc=(sV, 1))
                    evac_ref[mt] = (sV, sV.n)
            conv_ready = sS.n
            gate_ready = sS.n

            p.wait("tensor", sV, evac_ref[80][1])
            p.op("tensor", "matmul", pR[0:64, 0:512], dtwT[0:64, 0:64], Adt[:],
                 start=True, stop=True, inc=(sP, 1))
            p.wait("vector", sP)
            p.op("vector", "tensor_scalar", dt1[:], pR[0:64, 0:512],
                 dtb_col[0:64, 0:1], None, ALU.add)
            p.op("vector", "tensor_scalar", dt2[:], dt1[:], -1.0, None, ALU.mult)
            p.op("vector", "tensor_tensor", dt2[:], dt1[:], dt2[:], ALU.max)
            p.op("vector", "tensor_scalar", flv[:, 256:512], flv[:, 0:256], 1.0, None, ALU.mult)
            p.op("vector", "tensor_scalar", flv[:, 256:512], flv[:, 0:256], 1.0, None, ALU.mult, inc=(sV, 1))
            p.wait("scalar", sV)
            p.op("scalar", "activation", dt3[:], dt2[:], AF.Exp, scale=-1.0)
            p.op("scalar", "activation", dt2[:], dt3[:], AF.Ln, bias=1.0)
            p.op("scalar", "activation", fla[:, 64:128], fla[:, 0:64], AF.Copy)
            p.op("scalar", "activation", fla[:, 64:128], fla[:, 0:64], AF.Copy, inc=(sA, 1))
            p.wait("vector", sA)
            p.op("vector", "tensor_scalar", dt3[:], dt1[:], 0.0, None, ALU.max)
            p.op("vector", "tensor_tensor", dt1[:], dt3[:], dt2[:], ALU.add)
            p.op("vector", "tensor_scalar", Afm[:], dt1[:], neA_col[0:64, 0:1], None,
                 ALU.mult, inc=(sV, 1))

            dtctx.close()
            chctx = contextlib.ExitStack()
            CE = chctx.enter_context
            convbuf = CE(nc.sbuf_tensor(f"cvb{t}", [128, CONV], BF16))
            convscr = CE(nc.sbuf_tensor(f"cscr{t}", [128, CONV], BF16))
            macA = CE(nc.sbuf_tensor(f"mca{t}", [128, 128], F32))
            macB = CE(nc.sbuf_tensor(f"mcb{t}", [128, 128], F32))
            Xtm = CE(nc.sbuf_tensor(f"xtm{t}", [128, DI], BF16))
            Xdf = convscr[:, 0:DI]
            Btm = CE(nc.sbuf_tensor(f"btm{t}", [128, GN], BF16))
            Zm = CE(nc.sbuf_tensor(f"zm{t}", [128, 1024], BF16))
            Lsb = CE(nc.sbuf_tensor(f"lsb{t}", [128, 1024], BF16))
            Msb = CE(nc.sbuf_tensor(f"msb{t}", [128, 2 * 1024], BF16))
            Ytm = CE(nc.sbuf_tensor(f"ytm{t}", [128, DI], F32))
            scrY = CE(nc.sbuf_tensor(f"scy{t}", [128, 512], F32))
            scrY2 = CE(nc.sbuf_tensor(f"scy2{t}", [128, 512], F32))
            sgbuf = CE(nc.sbuf_tensor(f"sgb{t}", [128, DI], BF16))
            if True:

                a_prev = mcarry.get("a")
                y_done_prev = mcarry.get("y")
                conv_prev = mcarry.get("conv")
                for tc in range(NCH):
                    gc = t * NCH + tc
                    par = tc % 2
                    csb3 = conv_sb[:].rearrange("pp (ci w) -> pp ci w", ci=NCC)
                    if gc > 0:
                        p.op("vector", "tensor_copy", csb3[:, :, 0:3],
                             csb3[:, :, 128:131], inc=(sV, 1))
                    tailv = sV.n
                    p.wait("gpsimd", sS, conv_ready)
                    p.wait("gpsimd", sV, tailv)
                    if conv_prev is not None:
                        p.wait("gpsimd", sV, conv_prev)
                    p.op("gpsimd", "dma_start",
                         out=csb3[:, :, 3:131],
                         in_=s_conv[:, :, gc * 128:(gc + 1) * 128].transpose([1, 0, 2]),
                         inc=(sG, 16))
                    convld = sG.n
                    if a_prev is not None:
                        p.wait_snap("vector", a_prev)
                    p.op("vector", "tensor_tensor_scan", a2[0:64, :],
                         Afm[:, tc * 128:(tc + 1) * 128],
                         Afm[:, tc * 128:(tc + 1) * 128], 0.0, ALU.add, ALU.bypass,
                         inc=(sV, 1))
                    p.wait("vector", sV)
                    p.op("vector", "memset", spc[:], 0.0)
                    p.op("vector", "tensor_scalar", rcfm[:], a2[0:64, :],
                         a2[0:64, 127:128], -1.0, ALU.subtract, ALU.mult)
                    p.op("vector", "tensor_scalar", a2[64:128, :], a2[0:64, :], -1.0,
                         None, ALU.mult, inc=(sV, 1))
                    av = sV.n
                    p.wait("gpsimd", sV, av)
                    p.op("gpsimd", "dma_start", out=s_a2[gc], in_=a2[:], inc=(sS, 16))
                    a2st = sS.n
                    p.wait("tensor", sV, av)
                    p.op("tensor", "transpose", pR[:, 512:576], a2[0:64, :],
                         identf[0:64, 0:64])
                    p.op("tensor", "transpose", pR[:, 576:640], rcfm[:],
                         identf[0:64, 0:64], inc=(sP, 1))
                    trv = sP.n
                    p.wait("scalar", sP, trv)
                    p.op("scalar", "activation", expacs[:], pR[:, 512:576], AF.Exp)
                    p.op("scalar", "activation", exprc[:], pR[:, 576:640], AF.Exp)
                    p.op("scalar", "activation", c128a[0:64, 0:1], a2[0:64, 127:128],
                         AF.Exp)
                    p.op("scalar", "activation", fla[:, 64:128], fla[:, 0:64], AF.Copy)
                    p.op("scalar", "activation", fla[:, 64:128], fla[:, 0:64], AF.Copy, inc=(sA, 1))
                    p.wait("tensor", sA)
                    p.op("tensor", "transpose", pR[0:1, 640:704], c128a[0:64, 0:1],
                         identf[0:64, 0:64], inc=(sP, 1))
                    p.wait("vector", sP)
                    p.op("vector", "tensor_copy", cdrow[:], pR[0:1, 640:704])
                    p.op("vector", "tensor_scalar", flv[:, 256:512], flv[:, 0:256], 1.0, None, ALU.mult)
                    p.op("vector", "tensor_scalar", flv[:, 256:512], flv[:, 0:256], 1.0, None, ALU.mult, inc=(sV, 1))
                    p.wait("tensor", sV)
                    p.op("tensor", "matmul", pR[:, 704:768], onesf_row[:], cdrow[:],
                         start=True, stop=True, inc=(sP, 1))
                    p.wait("vector", sP)
                    p.op("vector", "tensor_copy", CDsb[:], pR[:, 704:768], inc=(sV, 1))
                    a_prev = [(sS, a2st), (sP, sP.n), (sA, sA.n)]

                    p.wait("vector", sG, convld)
                    if y_done_prev is not None:
                        p.wait("vector", sP, y_done_prev)
                    for ci in range(NCC):
                        base = ci * 132
                        p.op("vector", "tensor_scalar", macA[:],
                             conv_sb[:, base + 0: base + 128],
                             convw_c[:, ci * 4 + 0: ci * 4 + 1],
                             convb_c[:, ci:ci + 1], ALU.mult, ALU.add)
                        p.op("vector", "scalar_tensor_tensor", macB[:],
                             conv_sb[:, base + 1: base + 129],
                             convw_c[:, ci * 4 + 1: ci * 4 + 2], macA[:],
                             ALU.mult, ALU.add)
                        p.op("vector", "scalar_tensor_tensor", macA[:],
                             conv_sb[:, base + 2: base + 130],
                             convw_c[:, ci * 4 + 2: ci * 4 + 3], macB[:],
                             ALU.mult, ALU.add)
                        p.op("vector", "scalar_tensor_tensor",
                             convbuf[:, ci * 128:(ci + 1) * 128],
                             conv_sb[:, base + 3: base + 131],
                             convw_c[:, ci * 4 + 3: ci * 4 + 4], macA[:],
                             ALU.mult, ALU.add,
                             inc=(sV, 1) if ci == NCC - 1 else None)
                    convv = sV.n
                    conv_prev = convv
                    p.wait("scalar", sV, convv)
                    p.op("scalar", "activation", convscr[:], convbuf[:], AF.Square,
                         inc=(sA, 1))
                    p.wait("tensor", sA)
                    for ci in range(NCC):
                        p.op("tensor", "matmul", pR[0:1, 768:896], onesb_col[:],
                             convscr[:, ci * 128:(ci + 1) * 128],
                             start=(ci == 0), stop=(ci == NCC - 1),
                             inc=(sP, 1) if ci == NCC - 1 else None)
                    p.wait("vector", sP)
                    p.op("vector", "tensor_scalar", rsr[0:1, 0:128], pR[0:1, 768:896],
                         1.0 / CONV, 1e-6, ALU.mult, ALU.add)
                    p.op("vector", "tensor_scalar", flv[:, 256:512], flv[:, 0:256], 1.0, None, ALU.mult)
                    p.op("vector", "tensor_scalar", flv[:, 256:512], flv[:, 0:256], 1.0, None, ALU.mult, inc=(sV, 1))
                    p.wait("scalar", sV)
                    p.op("scalar", "activation", rsr[0:1, 128:256], rsr[0:1, 0:128],
                         AF.Sqrt)
                    p.op("scalar", "activation", fla[:, 64:128], fla[:, 0:64], AF.Copy)
                    p.op("scalar", "activation", fla[:, 64:128], fla[:, 0:64], AF.Copy, inc=(sA, 1))
                    p.wait("vector", sA)
                    p.op("vector", "reciprocal", rsr[0:1, 192:256].bitcast(F32) if False else rsr[0:1, 0:128], rsr[0:1, 128:256])
                    p.op("vector", "tensor_scalar", flv[:, 256:512], flv[:, 0:256], 1.0, None, ALU.mult)
                    p.op("vector", "tensor_scalar", flv[:, 256:512], flv[:, 0:256], 1.0, None, ALU.mult, inc=(sV, 1))
                    p.wait("tensor", sV)
                    p.op("tensor", "matmul", pR[:, 896:1024], onesf_row[:],
                         rsr[0:1, 0:128], start=True, stop=True, inc=(sP, 1))
                    p.wait("vector", sP)
                    for ci in range(NCC):
                        p.op("vector", "scalar_tensor_tensor",
                             convscr[:, ci * 128:(ci + 1) * 128],
                             convbuf[:, ci * 128:(ci + 1) * 128],
                             convnw_c[:, ci:ci + 1], pR[:, 896:1024],
                             ALU.mult, ALU.mult,
                             inc=(sV, 1) if ci == NCC - 1 else None)
                    p.wait("scalar", sV)
                    p.op("scalar", "activation", convbuf[:], convscr[:], AF.Silu,
                         inc=(sA, 1))
                    convoutv = sA.n

                    p.wait("tensor", sA, convoutv)
                    tr_ev = []
                    for b in range(5):
                        dst = pT1 if b % 2 == 0 else pT2
                        if b >= 2:
                            p.wait("tensor", sV, tr_ev[b - 2])
                        for j in range(8):
                            idx = b * 8 + j
                            p.op("tensor", "transpose", dst[:, j * 128:(j + 1) * 128],
                                 convbuf[:, idx * 128:(idx + 1) * 128], identb[:],
                                 inc=(sP, 1) if j == 7 else None)
                        p.wait("vector", sP)
                        for j in range(8):
                            idx = b * 8 + j
                            tgt = Xtm[:, idx * 128:(idx + 1) * 128] if idx < 32 else \
                                Btm[:, (idx - 32) * 128:(idx - 31) * 128]
                            p.op("vector", "tensor_copy", tgt,
                                 dst[:, j * 128:(j + 1) * 128],
                                 inc=(sV, 1) if j == 7 else None)
                        tr_ev.append(sV.n)
                    p.op("vector", "tensor_tensor",
                         Xdf.rearrange("pp (h dd) -> pp h dd", h=64),
                         Xtm[:].rearrange("pp (h dd) -> pp h dd", h=64),
                         exprc[:].unsqueeze(2).broadcast_to([128, 64, 64]),
                         ALU.mult, inc=(sV, 1))

                    zmv = {}
                    for wv in range(2):
                        p.wait("tensor", sA, convoutv)
                        if wv == 1:
                            p.wait("tensor", sV, zmv[0])
                        for g in range(wv * 4, wv * 4 + 4):
                            p.op("tensor", "matmul",
                                 pmm[3][:, (g % 4) * 128:(g % 4 + 1) * 128],
                                 convbuf[:, (32 + g) * 128:(33 + g) * 128],
                                 convbuf[:, (40 + g) * 128:(41 + g) * 128],
                                 start=True, stop=True,
                                 inc=(sP, 1) if g % 4 == 3 else None)
                        p.wait("vector", sP)
                        for g in range(wv * 4, wv * 4 + 4):
                            p.op("vector", "tensor_tensor", Zm[:, g * 128:(g + 1) * 128],
                                 pmm[3][:, (g % 4) * 128:(g % 4 + 1) * 128], maskz[:],
                                 ALU.mult, inc=(sV, 1) if g % 4 == 3 else None)
                        zmv[wv] = sV.n

                    stg_ld = [None] * 8
                    r_done = [None] * 8

                    def load_stg(gg):
                        buf = gg % 2
                        if gg >= 2:
                            p.wait("sync", sP, r_done[gg - 2])
                        else:
                            p.wait("sync", sP)
                        p.wait("sync", sS, a2st)
                        p.op("sync", "dma_start", out=stgP[buf][0:1, :],
                             in_=s_a2[gc][gg * 8:(gg + 1) * 8, :].rearrange(
                                 "pp f -> (pp f)").unsqueeze(0), inc=(sW, 16))
                        p.op("sync", "dma_start", out=stgN[buf][1:2, :],
                             in_=s_a2[gc][64 + gg * 8: 64 + (gg + 1) * 8, :].rearrange(
                                 "pp f -> (pp f)").unsqueeze(0), inc=(sW, 16))
                        stg_ld[gg] = sW.n

                    load_stg(0)
                    load_stg(1)

                    y_ev = None
                    for g in range(8):
                        buf = g % 2
                        p.wait("tensor", sW, stg_ld[g])
                        p.wait("tensor", sA)
                        for hh in range(8):
                            w0 = hh * 128
                            p.op("tensor", "matmul", pR[:, hh * 128:(hh + 1) * 128],
                                 stgN[buf][:, w0:w0 + 128], stgP[buf][:, w0:w0 + 128],
                                 start=True, stop=False)
                            p.op("tensor", "matmul", pR[:, hh * 128:(hh + 1) * 128],
                                 negbigi[:], lt1[:], start=False, stop=True,
                                 inc=(sP, 1) if hh == 7 else None)
                        r_done[g] = sP.n
                        if g + 2 < 8:
                            load_stg(g + 2)
                        p.wait("scalar", sP, r_done[g])
                        if g >= 1:
                            p.wait("scalar", sV)
                        p.op("scalar", "activation", Lsb[:, 0:1024], pR[:],
                             AF.Exp, inc=(sA, 1))
                        lv = sA.n
                        p.wait("vector", sA, lv)
                        p.wait("vector", sV, zmv[g // 4])
                        p.op("vector", "tensor_tensor",
                             Msb[:, (g % 2) * 1024:(g % 2 + 1) * 1024].rearrange(
                                 "pp (a b) -> pp a b", a=8),
                             Lsb[:, 0:1024].rearrange(
                                 "pp (a b) -> pp a b", a=8),
                             Zm[:, g * 128:(g + 1) * 128].unsqueeze(1).broadcast_to(
                                 [128, 8, 128]),
                             ALU.mult)
                        p.op("vector", "tensor_scalar", flv[:, 256:512], flv[:, 0:256], 1.0, None, ALU.mult)
                        p.op("vector", "tensor_scalar", flv[:, 256:512], flv[:, 0:256], 1.0, None, ALU.mult, inc=(sV, 1))
                        mvv = sV.n
                        p.wait("tensor", sV, mvv)
                        if y_ev is not None:
                            p.wait("tensor", sV, y_ev)
                        for hh in range(8):
                            h = g * 8 + hh
                            p.op("tensor", "matmul", pmm[0][:, hh * 64:(hh + 1) * 64],
                                 Msb[:, (g % 2) * 1024 + hh * 128:
                                     (g % 2) * 1024 + (hh + 1) * 128],
                                 Xtm[:, h * 64:(h + 1) * 64], start=True, stop=True)
                        p.op("tensor", "matmul", pmm[1][:],
                             convbuf[:, (40 + g) * 128:(41 + g) * 128],
                             hb_st[:, g * 512:(g + 1) * 512], start=True, stop=True)
                        p.op("tensor", "matmul", pmm[2][:],
                             Btm[:, g * 128:(g + 1) * 128],
                             Xdf[:, g * 512:(g + 1) * 512], start=True, stop=True,
                             inc=(sP, 1))
                        yv = sP.n
                        p.wait("vector", sP, yv)
                        p.op("vector", "tensor_tensor",
                             scrY[:].rearrange("pp (a b) -> pp a b", a=8),
                             pmm[1][:].rearrange("pp (a b) -> pp a b", a=8),
                             expacs[:, g * 8:(g + 1) * 8].unsqueeze(2).broadcast_to(
                                 [128, 8, 64]), ALU.mult)
                        p.op("vector", "tensor_tensor", Ytm[:, g * 512:(g + 1) * 512],
                             pmm[0][:], scrY[:], ALU.add)
                        p.op("vector", "tensor_tensor",
                             scrY2[:].rearrange("pp (a b) -> pp a b", a=8),
                             h_st[:, g * 512:(g + 1) * 512].rearrange(
                                 "pp (a b) -> pp a b", a=8),
                             CDsb[:, g * 8:(g + 1) * 8].unsqueeze(2).broadcast_to(
                                 [128, 8, 64]), ALU.mult)
                        p.op("vector", "tensor_tensor", h_st[:, g * 512:(g + 1) * 512],
                             pmm[2][:], scrY2[:], ALU.add)
                        p.op("vector", "tensor_copy", hb_st[:, g * 512:(g + 1) * 512],
                             h_st[:, g * 512:(g + 1) * 512], inc=(sV, 1))
                        y_ev = sV.n

                    p.wait("scalar", sV, y_ev)
                    p.op("scalar", "activation", Xdf, Ytm[:], AF.Square,
                         accum_out=c128a[:, 0:1])
                    p.op("scalar", "activation", fla[:, 64:128], fla[:, 0:64], AF.Copy)
                    p.op("scalar", "activation", fla[:, 64:128], fla[:, 0:64], AF.Copy, inc=(sA, 1))
                    p.wait("vector", sA)
                    p.op("vector", "tensor_scalar", c128a[:, 4:5], c128a[:, 0:1],
                         1.0 / DI, 1e-6, ALU.mult, ALU.add)
                    p.op("vector", "tensor_scalar", flv[:, 256:512], flv[:, 0:256], 1.0, None, ALU.mult)
                    p.op("vector", "tensor_scalar", flv[:, 256:512], flv[:, 0:256], 1.0, None, ALU.mult, inc=(sV, 1))
                    p.wait("scalar", sV)
                    p.op("scalar", "activation", c128a[:, 6:7], c128a[:, 4:5], AF.Sqrt)
                    p.op("scalar", "activation", fla[:, 64:128], fla[:, 0:64], AF.Copy)
                    p.op("scalar", "activation", fla[:, 64:128], fla[:, 0:64], AF.Copy, inc=(sA, 1))
                    p.wait("vector", sA)
                    p.op("vector", "reciprocal", c128a[:, 7:8], c128a[:, 6:7])
                    p.op("vector", "tensor_scalar", flv[:, 256:512], flv[:, 0:256], 1.0, None, ALU.mult)
                    p.op("vector", "tensor_scalar", flv[:, 256:512], flv[:, 0:256], 1.0, None, ALU.mult)
                    p.op("vector", "tensor_scalar", Xtm[:], Ytm[:], c128a[:, 7:8],
                         None, ALU.mult, inc=(sV, 1))
                    ynv = sV.n
                    p.wait("gpsimd", sS, gate_ready)
                    if tc >= 1:
                        p.wait("gpsimd", sV)
                    p.op("gpsimd", "dma_start",
                         out=sgbuf[:].rearrange("pp (ic w) -> pp ic w", ic=NIC),
                         in_=s_gate[:, :, tc * 128:(tc + 1) * 128].transpose([1, 0, 2]),
                         inc=(sG, 16))
                    sgv = sG.n
                    p.wait("tensor", sV, ynv)
                    ytr_ev = []
                    for b in range(4):
                        dst = pT1 if b % 2 == 0 else pT2
                        if b >= 2:
                            p.wait("tensor", sV, ytr_ev[b - 2])
                        for j in range(8):
                            ic = b * 8 + j
                            p.op("tensor", "transpose", dst[:, j * 128:(j + 1) * 128],
                                 Xtm[:, ic * 128:(ic + 1) * 128], identb[:],
                                 inc=(sP, 1) if j == 7 else None)
                        p.wait("vector", sP)
                        p.wait("vector", sG, sgv)
                        for j in range(8):
                            ic = b * 8 + j
                            p.op("vector", "tensor_tensor",
                                 Yfm[:, ic * 512 + tc * 128: ic * 512 + (tc + 1) * 128],
                                 dst[:, j * 128:(j + 1) * 128],
                                 sgbuf[:, ic * 128:(ic + 1) * 128], ALU.mult,
                                 inc=(sV, 1) if j == 7 else None)
                        ytr_ev.append(sV.n)
                    y_done_prev = sP.n
                    if heavy and t == 1:
                        p.wait("gpsimd", sA, convoutv)
                        p.op("gpsimd", "dma_start", out=d_dcv[tc], in_=convbuf[:],
                             inc=(sS, 16))
                        p.wait("gpsimd", sV, y_ev)
                        p.op("gpsimd", "dma_start", out=d_dyt[tc], in_=Ytm[:],
                             inc=(sS, 16))
                        p.op("gpsimd", "dma_start", out=d_dh[tc], in_=h_st[:],
                             inc=(sS, 16))
                        p.op("gpsimd", "dma_start", out=d_ddk[tc, :, 0:64],
                             in_=CDsb[:], inc=(sS, 16))
                        p.op("gpsimd", "dma_start",
                             out=d_ddk[tc, :, 64:128].bitcast(BF16)[:, 0:64],
                             in_=expacs[:], inc=(sS, 16))
                        p.op("gpsimd", "dma_start",
                             out=d_ddk[tc, :, 128:192].bitcast(BF16)[:, 0:64],
                             in_=exprc[:], inc=(sS, 16))
                        p.op("gpsimd", "dma_start", out=d_dlm[tc, :, 0:1024],
                             in_=Lsb[:, 0:1024], inc=(sS, 16))
                        p.op("gpsimd", "dma_start", out=d_dlm[tc, :, 1024:2048],
                             in_=Msb[:, 1024:2048], inc=(sS, 16))
                    pw_step(20)

            mcarry["a"] = a_prev
            mcarry["y"] = y_done_prev
            mcarry["conv"] = conv_prev
            if heavy:
                p.wait("gpsimd", sV)
                p.op("gpsimd", "dma_start", out=d_dyf[t], in_=Yfm[:], inc=(sS, 16))
            chctx.close()
            with nc.sbuf_tensor(f"wob{t}", [128, 4 * 512], BF16) as wopbuf, \
                 nc.sbuf_tensor(f"zst{t}", [128, 2 * 512], F32) as zstage:
                p.wait("sync", sV)
                p.wait("sync", sA)
                op_mm = {}
                z_st = []
                for oq in range(4):
                    for ic in range(NIC):
                        n_ = oq * NIC + ic
                        i4 = n_ % 4
                        if n_ >= 4:
                            p.wait("sync", sP, op_mm[n_ - 4])
                        else:
                            p.wait("sync", sP)
                        p.op("sync", "dma_start",
                             out=wopbuf[:, i4 * 512:(i4 + 1) * 512],
                             in_=s_wop[ic][:, oq * 512:(oq + 1) * 512], inc=(sW, 16))
                        ldv = sW.n
                        p.wait("tensor", sW, ldv)
                        if n_ == 0:
                            p.wait("tensor", sV)
                        if ic == 0 and oq > 0:
                            p.wait("tensor", sV, z_ev)
                        for ts_ in range(4):
                            p.op("tensor", "matmul", pmm[ts_][:],
                                 Yfm[:, ic * 512 + ts_ * 128: ic * 512 + (ts_ + 1) * 128],
                                 wopbuf[:, i4 * 512:(i4 + 1) * 512],
                                 start=(ic == 0), stop=(ic == NIC - 1),
                                 inc=(sP, 1) if ts_ == 3 else None)
                        op_mm[n_] = sP.n
                    p.wait("vector", sP, op_mm[oq * NIC + NIC - 1])
                    for ts_ in range(4):
                        if len(z_st) >= 2:
                            p.wait("vector", sS, z_st[-2])
                        p.op("vector", "tensor_copy",
                             zstage[:, (ts_ % 2) * 512:(ts_ % 2 + 1) * 512],
                             pmm[ts_][:], inc=(sV, 1))
                        p.wait("gpsimd", sV)
                        p.op("gpsimd", "dma_start",
                             out=s_z[t * TMT + ts_ * 128: t * TMT + (ts_ + 1) * 128, oq * 512:(oq + 1) * 512],
                             in_=zstage[:, (ts_ % 2) * 512:(ts_ % 2 + 1) * 512],
                             inc=(sS, 16))
                        z_st.append(sS.n)
                    z_ev = sV.n
                z_ready = sS.n
                pw_step(26)

        with nc.sbuf_tensor(f"zrd{t}", [128, 2 * 2048], F32) as zrd, \
             nc.sbuf_tensor(f"xr2{t}", [128, 2 * 2048], F32) as xrd, \
             nc.sbuf_tensor(f"zf{t}", [128, 2048], F32) as zf, \
             nc.sbuf_tensor(f"xtm2{t}", [128, 2048], BF16) as xtmb2, \
             nc.sbuf_tensor(f"x1s{t}", [128, 2048], BF16) as x1stage:
            x1f_st = [0] * 4
            x1b_st = [0] * 4
            tr_done = [0] * 4
            for ts_ in range(4):
                par = ts_ % 2
                p.wait("gpsimd", sS, z_ready)
                if ts_ >= 2:
                    p.wait("gpsimd", sV)
                    p.wait("gpsimd", sS, x1f_st[ts_ - 2])
                p.op("gpsimd", "dma_start", out=zrd[:, par * 2048:(par + 1) * 2048],
                     in_=s_z[t * TMT + ts_ * 128: t * TMT + (ts_ + 1) * 128, :], inc=(sG, 16))
                p.op("gpsimd", "dma_start", out=xrd[:, par * 2048:(par + 1) * 2048],
                     in_=d_x[t * TMT + ts_ * 128: t * TMT + (ts_ + 1) * 128, :],
                     inc=(sG, 16))
                ldv = sG.n
                p.wait("vector", sG, ldv)
                if ts_ >= 1:
                    p.wait("vector", sA)
                    p.wait("vector", sS, x1f_st[ts_ - 1])
                p.op("vector", "scalar_tensor_tensor", zf[:],
                     xrd[:, par * 2048:(par + 1) * 2048], 2.0,
                     zrd[:, par * 2048:(par + 1) * 2048], ALU.mult, ALU.add)
                if heavy and t == 99 and ts_ == 0:
                    p.op("vector", "tensor_reduce", snapb[:, 12:13],
                         zrd[:, par * 2048:(par + 1) * 2048], AXX, ALU.add)
                    p.op("vector", "tensor_reduce", snapb[:, 13:14],
                         xrd[:, par * 2048:(par + 1) * 2048], AXX, ALU.add)
                p.op("vector", "tensor_reduce", c128a[:, 0:1], zf[:], AXX, ALU.add,
                     inc=(sV, 1))
                p.wait("scalar", sV)
                if ts_ >= 1:
                    p.wait("scalar", sP, tr_done[ts_ - 1])
                p.op("scalar", "activation", xtmb2[:], zf[:], AF.Square)
                p.op("scalar", "activation", fla[:, 64:128], fla[:, 0:64], AF.Copy)
                p.op("scalar", "activation", fla[:, 64:128], fla[:, 0:64], AF.Copy, inc=(sA, 1))
                p.wait("vector", sA)
                p.op("vector", "tensor_reduce", c128a[:, 1:2], xtmb2[:], AXX, ALU.add)
                p.op("vector", "tensor_scalar", flv[:, 256:512], flv[:, 0:256], 1.0, None, ALU.mult)
                p.op("vector", "tensor_scalar", flv[:, 256:512], flv[:, 0:256], 1.0, None, ALU.mult)
                p.op("vector", "tensor_scalar", c128a[:, 2:3], c128a[:, 0:1], 1.0 / D,
                     None, ALU.mult)
                p.op("vector", "tensor_scalar", c128a[:, 4:5], c128a[:, 1:2], 1.0 / D,
                     1e-5, ALU.mult, ALU.add)
                p.op("vector", "tensor_scalar", flv[:, 256:512], flv[:, 0:256], 1.0, None, ALU.mult)
                p.op("vector", "tensor_scalar", flv[:, 256:512], flv[:, 0:256], 1.0, None, ALU.mult)
                p.op("vector", "tensor_tensor", c128a[:, 3:4], c128a[:, 2:3],
                     c128a[:, 2:3], ALU.mult)
                p.op("vector", "tensor_scalar", flv[:, 256:512], flv[:, 0:256], 1.0, None, ALU.mult)
                p.op("vector", "tensor_scalar", flv[:, 256:512], flv[:, 0:256], 1.0, None, ALU.mult)
                p.op("vector", "tensor_tensor", c128a[:, 5:6], c128a[:, 4:5],
                     c128a[:, 3:4], ALU.subtract)
                p.op("vector", "tensor_scalar", flv[:, 256:512], flv[:, 0:256], 1.0, None, ALU.mult)
                p.op("vector", "tensor_scalar", flv[:, 256:512], flv[:, 0:256], 1.0, None, ALU.mult, inc=(sV, 1))
                p.wait("scalar", sV)
                p.op("scalar", "activation", c128a[:, 6:7], c128a[:, 5:6], AF.Sqrt)
                p.op("scalar", "activation", fla[:, 64:128], fla[:, 0:64], AF.Copy)
                p.op("scalar", "activation", fla[:, 64:128], fla[:, 0:64], AF.Copy, inc=(sA, 1))
                p.wait("vector", sA)
                p.op("vector", "reciprocal", c128a[:, 7:8], c128a[:, 6:7],
                     inc=(sV, 1))
                p.op("vector", "tensor_scalar", xrd[:, par * 2048:(par + 1) * 2048],
                     zf[:], c128a[:, 2:3], None, ALU.subtract)
                p.op("vector", "tensor_scalar", zf[:],
                     xrd[:, par * 2048:(par + 1) * 2048], c128a[:, 7:8], None,
                     ALU.mult, inc=(sV, 1))
                x1v = sV.n
                p.wait("gpsimd", sV, x1v)
                p.op("gpsimd", "dma_start",
                     out=s_x1f[t * TMT + ts_ * 128: t * TMT + (ts_ + 1) * 128, :],
                     in_=zf[:], inc=(sS, 16))
                x1f_st[ts_] = sS.n
                p.op("vector", "tensor_copy", xtmb2[:], zf[:], inc=(sV, 1))
                p.wait("tensor", sV)
                for kc in range(16):
                    dst = pT1 if kc < 8 else pT2
                    p.op("tensor", "transpose", dst[:, (kc % 8) * 128:(kc % 8 + 1) * 128],
                         xtmb2[:, kc * 128:(kc + 1) * 128], identb[:],
                         inc=(sP, 1) if kc % 8 == 7 else None)
                tr_done[ts_] = sP.n
                p.wait("vector", sP)
                if ts_ >= 1:
                    p.wait("vector", sS, x1b_st[ts_ - 1])
                for kc in range(16):
                    src = pT1 if kc < 8 else pT2
                    p.op("vector", "tensor_copy", x1stage[:, kc * 128:(kc + 1) * 128],
                         src[:, (kc % 8) * 128:(kc % 8 + 1) * 128],
                         inc=(sV, 1) if kc == 15 else None)
                p.wait("gpsimd", sV)
                p.op("gpsimd", "dma_start", out=s_x1b[t * NCH + ts_],
                     in_=x1stage[:], inc=(sS, 16))
                x1b_st[ts_] = sS.n
        mcarry["x1v"] = x1v
        pw_step(26)
        tile_ctx.close()

    pw_step(10 ** 9)
    barrier()
    mctx.close()

    fctx = contextlib.ExitStack()
    FE = fctx.enter_context
    x1fm = FE(nc.sbuf_tensor("x1fm", [128, KC * 512], BF16))
    hbuf = FE(nc.sbuf_tensor("hbuf", [128, NEI * 512], BF16))
    gbuf = FE(nc.sbuf_tensor("gbuf", [128, 2 * 2048], BF16))
    ubuf = FE(nc.sbuf_tensor("ubuf", [128, 2 * 2048], BF16))
    wdbuf = FE(nc.sbuf_tensor("wdb", [128, 4 * 512], BF16))
    lgt = FE(nc.sbuf_tensor("lgt", [8, 512], F32))
    lgtm = FE(nc.sbuf_tensor("lgtm", [128, 32], F32))
    etile = FE(nc.sbuf_tensor("etile", [128, 8], F32))
    eqt = FE(nc.sbuf_tensor("eqt", [128, 8], F32))
    e2t = FE(nc.sbuf_tensor("e2t", [128, 8], F32))
    combtm = FE(nc.sbuf_tensor("combtm", [128, 32], F32))
    combfm = FE(nc.sbuf_tensor("combfm", [8, 512], F32))
    wsb = FE(nc.sbuf_tensor("wsb", [128, 512], F32))
    sgb2 = FE(nc.sbuf_tensor("fsgb", [128, 512], BF16))
    tmpf = FE(nc.sbuf_tensor("tmpf", [128, 512], F32))
    z2b = FE(nc.sbuf_tensor("z2b", [128, 4 * 2048], F32))
    x1rd = FE(nc.sbuf_tensor("x1r", [128, 2 * 2048], F32))
    zf2 = FE(nc.sbuf_tensor("fzf", [128, 2048], F32))
    outstage = FE(nc.sbuf_tensor("outs", [128, 2048], F32))
    if True:

        x1ld = {}
        rcombv = {}
        gu_last2 = None
        z2v_prev = None
        down_hist = []

        def emit_x1_loads(t, wait_sp):
            if wait_sp is not None:
                p.wait("gpsimd", sP, wait_sp)
            for kc in range(KC):
                p.op("gpsimd", "dma_start",
                     out=x1fm[:, kc * 512:(kc + 1) * 512].rearrange(
                         "pp (tc w) -> pp tc w", tc=NCH),
                     in_=s_x1b[t * NCH:(t + 1) * NCH, :, kc * 128:(kc + 1) * 128]
                     .transpose([1, 0, 2]), inc=(sG, 16))
            x1ld[t] = sG.n

        def router_head(t):
            p.wait("tensor", sG, x1ld[t])
            if t > 0:
                p.wait("tensor", sV, rcombv[t - 1])
            for kc in range(KC):
                p.op("tensor", "matmul", pR[0:8, 0:512], wrb[:, kc * 8:(kc + 1) * 8],
                     x1fm[:, kc * 512:(kc + 1) * 512], start=(kc == 0),
                     stop=(kc == 15), inc=(sP, 1) if kc == 15 else None)
            p.wait("vector", sP)
            p.op("vector", "tensor_copy", lgt[:], pR[0:8, 0:512])
            p.op("vector", "tensor_scalar", flv[:, 256:512], flv[:, 0:256], 1.0, None, ALU.mult)
            p.op("vector", "tensor_scalar", flv[:, 256:512], flv[:, 0:256], 1.0, None, ALU.mult, inc=(sV, 1))
            p.wait("tensor", sV)
            for ts_ in range(4):
                p.op("tensor", "transpose", pR[:, 512 + ts_ * 8: 512 + (ts_ + 1) * 8],
                     lgt[0:8, ts_ * 128:(ts_ + 1) * 128], identf[0:8, 0:8],
                     inc=(sP, 1) if ts_ == 3 else None)
            p.wait("vector", sP)
            p.op("vector", "tensor_copy", lgtm[:], pR[:, 512:544])
            p.op("vector", "tensor_scalar", flv[:, 256:512], flv[:, 0:256], 1.0, None, ALU.mult)
            p.op("vector", "tensor_scalar", flv[:, 256:512], flv[:, 0:256], 1.0, None, ALU.mult, inc=(sV, 1))

        def router_iter(t, ts_):
            sl = lgtm[:, ts_ * 8:(ts_ + 1) * 8]
            p.op("vector", "reduce_max", c128a[:, 0:1], sl, AXX)
            p.op("vector", "tensor_scalar", flv[:, 256:512], flv[:, 0:256], 1.0, None, ALU.mult)
            p.op("vector", "tensor_scalar", flv[:, 256:512], flv[:, 0:256], 1.0, None, ALU.mult)
            p.op("vector", "tensor_scalar", c128a[:, 2:3], c128a[:, 0:1], -1.0,
                 None, ALU.mult)
            p.op("vector", "tensor_scalar", flv[:, 256:512], flv[:, 0:256], 1.0, None, ALU.mult)
            p.op("vector", "tensor_scalar", flv[:, 256:512], flv[:, 0:256], 1.0, None, ALU.mult, inc=(sV, 1))
            p.wait("scalar", sV)
            p.op("scalar", "activation", etile[:], sl, AF.Exp,
                 bias=c128a[:, 2:3])
            p.op("scalar", "activation", fla[:, 64:128], fla[:, 0:64], AF.Copy)
            p.op("scalar", "activation", fla[:, 64:128], fla[:, 0:64], AF.Copy, inc=(sA, 1))
            p.wait("vector", sA)
            p.op("vector", "reduce_max", c128a[:, 1:2], etile[:], AXX)
            p.op("vector", "tensor_scalar", flv[:, 256:512], flv[:, 0:256], 1.0, None, ALU.mult)
            p.op("vector", "tensor_scalar", flv[:, 256:512], flv[:, 0:256], 1.0, None, ALU.mult)
            p.op("vector", "tensor_tensor", eqt[:], etile[:],
                 c128a[:, 1:2].broadcast_to([128, 8]), ALU.is_ge)
            p.op("vector", "tensor_scalar", flv[:, 256:512], flv[:, 0:256], 1.0, None, ALU.mult)
            p.op("vector", "tensor_scalar", flv[:, 256:512], flv[:, 0:256], 1.0, None, ALU.mult)
            p.op("vector", "scalar_tensor_tensor", e2t[:], eqt[:], -BIG, etile[:],
                 ALU.mult, ALU.add)
            p.op("vector", "tensor_scalar", flv[:, 256:512], flv[:, 0:256], 1.0, None, ALU.mult)
            p.op("vector", "tensor_scalar", flv[:, 256:512], flv[:, 0:256], 1.0, None, ALU.mult)
            p.op("vector", "reduce_max", c128a[:, 4:5], e2t[:], AXX)
            p.op("vector", "tensor_scalar", flv[:, 256:512], flv[:, 0:256], 1.0, None, ALU.mult)
            p.op("vector", "tensor_scalar", flv[:, 256:512], flv[:, 0:256], 1.0, None, ALU.mult)
            p.op("vector", "tensor_tensor", eqt[:], etile[:],
                 c128a[:, 4:5].broadcast_to([128, 8]), ALU.is_ge)
            p.op("vector", "tensor_scalar", flv[:, 256:512], flv[:, 0:256], 1.0, None, ALU.mult)
            p.op("vector", "tensor_scalar", flv[:, 256:512], flv[:, 0:256], 1.0, None, ALU.mult)
            p.op("vector", "tensor_tensor", c128a[:, 5:6], c128a[:, 1:2],
                 c128a[:, 4:5], ALU.add)
            p.op("vector", "tensor_scalar", flv[:, 256:512], flv[:, 0:256], 1.0, None, ALU.mult)
            p.op("vector", "tensor_scalar", flv[:, 256:512], flv[:, 0:256], 1.0, None, ALU.mult)
            p.op("vector", "reciprocal", c128a[:, 6:7], c128a[:, 5:6])
            p.op("vector", "tensor_scalar", flv[:, 256:512], flv[:, 0:256], 1.0, None, ALU.mult)
            p.op("vector", "tensor_scalar", flv[:, 256:512], flv[:, 0:256], 1.0, None, ALU.mult)
            p.op("vector", "tensor_tensor", e2t[:], etile[:], eqt[:], ALU.mult)
            p.op("vector", "tensor_scalar", flv[:, 256:512], flv[:, 0:256], 1.0, None, ALU.mult)
            p.op("vector", "tensor_scalar", flv[:, 256:512], flv[:, 0:256], 1.0, None, ALU.mult)
            p.op("vector", "tensor_tensor", combtm[:, ts_ * 8:(ts_ + 1) * 8],
                 e2t[:], c128a[:, 6:7].broadcast_to([128, 8]), ALU.mult)
            p.op("vector", "tensor_scalar", flv[:, 256:512], flv[:, 0:256], 1.0, None, ALU.mult)
            p.op("vector", "tensor_scalar", flv[:, 256:512], flv[:, 0:256], 1.0, None, ALU.mult, inc=(sV, 1))

        def router_tail(t):
            p.wait("tensor", sV)
            for ts_ in range(4):
                p.op("tensor", "transpose", pR[0:8, ts_ * 128:(ts_ + 1) * 128],
                     combtm[:, ts_ * 8:(ts_ + 1) * 8], identf[:],
                     inc=(sP, 1) if ts_ == 3 else None)
            p.wait("vector", sP)
            p.op("vector", "tensor_copy", combfm[:], pR[0:8, 0:512])
            p.op("vector", "tensor_scalar", flv[:, 256:512], flv[:, 0:256], 1.0, None, ALU.mult)
            p.op("vector", "tensor_scalar", flv[:, 256:512], flv[:, 0:256], 1.0, None, ALU.mult, inc=(sV, 1))
            rcombv[t] = sV.n

        emit_x1_loads(0, None)
        router_head(0)
        for _ts in range(4):
            router_iter(0, _ts)
        router_tail(0)

        for t in range(NTILE):
            combv = rcombv[t]
            gu_mm = {}
            glue_done = {}
            for e in range(EXP):
                p.wait("tensor", sV, combv)
                if e == 0:
                    if z2v_prev is not None:
                        p.wait("tensor", sV, z2v_prev)
                else:
                    p.wait("tensor", sV, glue_done[e * NIT - 1])
                p.op("tensor", "matmul", pmm[3][:], sel8[:, e * 128:(e + 1) * 128],
                     combfm[:], start=True, stop=True, inc=(sP, 1))
                p.wait("vector", sP)
                p.op("vector", "tensor_copy", wsb[:], pmm[3][:], inc=(sV, 1))
                wsbv = sV.n
                for it_ in range(NIT):
                    ei = e * NIT + it_
                    par = ei % 2
                    if ei >= 2:
                        p.wait("sync", sP, gu_mm[ei - 2])
                    elif gu_last2 is not None:
                        p.wait("sync", sP, gu_last2)
                    p.op("sync", "dma_start", out=gbuf[:, par * 2048:(par + 1) * 2048],
                         in_=s_wg[e][it_], inc=(sW, 16))
                    p.op("sync", "dma_start", out=ubuf[:, par * 2048:(par + 1) * 2048],
                         in_=s_wu[e][it_], inc=(sW, 16))
                    ldv = sW.n
                    p.wait("tensor", sW, ldv)
                    if ei >= 2:
                        p.wait("tensor", sV, glue_done[ei - 2])
                    if it_ == 1:
                        p.wait("tensor", sV, wsbv)
                    pg = pmm[0] if ei % 2 == 0 else pmm[2]
                    pu = pmm[1] if ei % 2 == 0 else pmm[3]
                    for kc in range(KC):
                        p.op("tensor", "matmul", pg[:],
                             gbuf[:, par * 2048 + kc * 128: par * 2048 + (kc + 1) * 128],
                             x1fm[:, kc * 512:(kc + 1) * 512], start=(kc == 0),
                             stop=(kc == 15))
                    for kc in range(KC):
                        p.op("tensor", "matmul", pu[:],
                             ubuf[:, par * 2048 + kc * 128: par * 2048 + (kc + 1) * 128],
                             x1fm[:, kc * 512:(kc + 1) * 512], start=(kc == 0),
                             stop=(kc == 15), inc=(sP, 1) if kc == 15 else None)
                    gu_mm[ei] = sP.n
                    p.wait("scalar", sP, gu_mm[ei])
                    p.op("scalar", "activation", sgb2[:], pg[:], AF.Silu)
                    p.op("scalar", "activation", fla[:, 64:128], fla[:, 0:64], AF.Copy)
                    p.op("scalar", "activation", fla[:, 64:128], fla[:, 0:64], AF.Copy, inc=(sA, 1))
                    p.wait("vector", sA)
                    p.op("vector", "tensor_tensor", tmpf[:], sgb2[:], pu[:], ALU.mult)
                    p.wait("vector", sV, wsbv)
                    p.op("vector", "tensor_tensor", hbuf[:, ei * 512:(ei + 1) * 512],
                         tmpf[:], wsb[:], ALU.mult, inc=(sV, 1))
                    glue_done[ei] = sV.n
            hv = sV.n
            gu_last2 = gu_mm[NEI - 1]

            if heavy and t == NTILE - 1:
                p.wait("gpsimd", sV, hv)
                p.op("gpsimd", "dma_start", out=d_dhb[:], in_=hbuf[:], inc=(sS, 16))
                p.op("gpsimd", "dma_start", out=d_dcb[:], in_=combfm[:], inc=(sS, 16))

            if t + 1 < NTILE:
                emit_x1_loads(t + 1, gu_mm[NEI - 1])

            for oq in range(4):
                for ei in range(NEI):
                    e, ic = divmod(ei, NIT)
                    i4 = ei % 4
                    if len(down_hist) >= 4:
                        p.wait("sync", sP, down_hist[-4])
                    p.op("sync", "dma_start", out=wdbuf[:, i4 * 512:(i4 + 1) * 512],
                         in_=s_wd[e][ic][:, oq * 512:(oq + 1) * 512], inc=(sW, 16))
                    ldv = sW.n
                    p.wait("tensor", sW, ldv)
                    if ei == 0:
                        p.wait("tensor", sV, hv)
                        if oq > 0:
                            p.wait("tensor", sV, z2_ev)
                    for ts_ in range(4):
                        p.op("tensor", "matmul", pmm[ts_][:],
                             hbuf[:, ei * 512 + ts_ * 128: ei * 512 + (ts_ + 1) * 128],
                             wdbuf[:, i4 * 512:(i4 + 1) * 512],
                             start=(ei == 0), stop=(ei == NEI - 1),
                             inc=(sP, 1) if ts_ == 3 else None)
                    down_hist.append(sP.n)
                p.wait("vector", sP, down_hist[-1])
                for ts_ in range(4):
                    p.op("vector", "tensor_copy",
                         z2b[:, ts_ * 2048 + oq * 512: ts_ * 2048 + (oq + 1) * 512],
                         pmm[ts_][:], inc=(sV, 1) if ts_ == 3 else None)
                z2_ev = sV.n
                if t + 1 < NTILE:
                    if oq == 0:
                        router_head(t + 1)
                    router_iter(t + 1, oq)
                    if oq == 3:
                        router_tail(t + 1)
            z2v = sV.n
            z2v_prev = z2v

            for ts_ in range(4):
                par = ts_ % 2
                if ts_ >= 2:
                    p.wait("gpsimd", sV)
                p.op("gpsimd", "dma_start", out=x1rd[:, par * 2048:(par + 1) * 2048],
                     in_=s_x1f[t * TMT + ts_ * 128: t * TMT + (ts_ + 1) * 128, :],
                     inc=(sG, 16))
                ldv = sG.n
                p.wait("vector", sG, ldv)
                p.wait("vector", sV, z2v)
                if ts_ >= 1:
                    p.wait("vector", sA)
                    p.wait("vector", sS, out_st_prev)
                p.op("vector", "tensor_tensor", zf2[:],
                     z2b[:, ts_ * 2048:(ts_ + 1) * 2048],
                     x1rd[:, par * 2048:(par + 1) * 2048], ALU.add)
                p.op("vector", "tensor_reduce", c128a[:, 0:1], zf2[:], AXX, ALU.add,
                     inc=(sV, 1))
                p.wait("scalar", sV)
                p.op("scalar", "activation", outstage.bitcast(BF16)[:, 0:2048], zf2[:],
                     AF.Square)
                p.op("scalar", "activation", fla[:, 64:128], fla[:, 0:64], AF.Copy)
                p.op("scalar", "activation", fla[:, 64:128], fla[:, 0:64], AF.Copy, inc=(sA, 1))
                p.wait("vector", sA)
                if ts_ >= 1:
                    p.wait("vector", sS, out_st_prev)
                p.op("vector", "tensor_reduce", c128a[:, 1:2],
                     outstage.bitcast(BF16)[:, 0:2048], AXX, ALU.add)
                p.op("vector", "tensor_scalar", flv[:, 256:512], flv[:, 0:256], 1.0, None, ALU.mult)
                p.op("vector", "tensor_scalar", flv[:, 256:512], flv[:, 0:256], 1.0, None, ALU.mult)
                p.op("vector", "tensor_scalar", c128a[:, 2:3], c128a[:, 0:1], 1.0 / D,
                     None, ALU.mult)
                p.op("vector", "tensor_scalar", c128a[:, 4:5], c128a[:, 1:2], 1.0 / D,
                     1e-5, ALU.mult, ALU.add)
                p.op("vector", "tensor_scalar", flv[:, 256:512], flv[:, 0:256], 1.0, None, ALU.mult)
                p.op("vector", "tensor_scalar", flv[:, 256:512], flv[:, 0:256], 1.0, None, ALU.mult)
                p.op("vector", "tensor_tensor", c128a[:, 3:4], c128a[:, 2:3],
                     c128a[:, 2:3], ALU.mult)
                p.op("vector", "tensor_scalar", flv[:, 256:512], flv[:, 0:256], 1.0, None, ALU.mult)
                p.op("vector", "tensor_scalar", flv[:, 256:512], flv[:, 0:256], 1.0, None, ALU.mult)
                p.op("vector", "tensor_tensor", c128a[:, 5:6], c128a[:, 4:5],
                     c128a[:, 3:4], ALU.subtract)
                p.op("vector", "tensor_scalar", flv[:, 256:512], flv[:, 0:256], 1.0, None, ALU.mult)
                p.op("vector", "tensor_scalar", flv[:, 256:512], flv[:, 0:256], 1.0, None, ALU.mult, inc=(sV, 1))
                p.wait("scalar", sV)
                p.op("scalar", "activation", c128a[:, 6:7], c128a[:, 5:6], AF.Sqrt)
                p.op("scalar", "activation", fla[:, 64:128], fla[:, 0:64], AF.Copy)
                p.op("scalar", "activation", fla[:, 64:128], fla[:, 0:64], AF.Copy, inc=(sA, 1))
                p.wait("vector", sA)
                p.op("vector", "reciprocal", c128a[:, 7:8], c128a[:, 6:7], inc=(sV, 1))
                p.op("vector", "tensor_scalar", outstage[:], zf2[:], c128a[:, 2:3],
                     None, ALU.subtract)
                p.op("vector", "tensor_scalar", zf2[:], outstage[:], c128a[:, 7:8],
                     None, ALU.mult, inc=(sV, 1))
                p.wait("gpsimd", sV)
                p.op("gpsimd", "dma_start",
                     out=d_out[t * TMT + ts_ * 128: t * TMT + (ts_ + 1) * 128, :],
                     in_=zf2[:], inc=(sS, 16))
                out_st_prev = sS.n

        barrier(("tensor", "vector", "scalar", "gpsimd"))

    fctx.close()
    if heavy:
        p.wait("gpsimd", sV)
        p.op("gpsimd", "dma_start", out=d_dln[:], in_=snapb[:], inc=(sS, 16))
    p.wait("gpsimd", sS)
    p.wait("gpsimd", sW)
    p.wait("gpsimd", sG)

    with nc.Block() as block:
        @block.tensor
        def _(eng):
            for fn in p.streams["tensor"]:
                fn(eng)

        @block.vector
        def _(eng):
            for fn in p.streams["vector"]:
                fn(eng)

        @block.scalar
        def _(eng):
            for fn in p.streams["scalar"]:
                fn(eng)

        @block.sync
        def _(eng):
            for fn in p.streams["sync"]:
                fn(eng)

        @block.gpsimd
        def _(eng):
            for fn in p.streams["gpsimd"]:
                fn(eng)

    ctx.close()
    return nc


_NC_CACHE = None


def _get_nc():
    global _NC_CACHE
    if _NC_CACHE is None:
        _NC_CACHE = build_program()
    return _NC_CACHE


def kernel(**inputs):
    xfull = np.ascontiguousarray(inputs["L0_attn_x"])
    consts = dict(
        identb=np.eye(128, dtype=ml_dtypes.bfloat16),
        identf=np.eye(128, dtype=np.float32),
        negbigi=(-BIG * np.eye(128)).astype(ml_dtypes.bfloat16),
        lt1=(np.arange(128)[None, :] < np.arange(128)[:, None]).astype(ml_dtypes.bfloat16),
        maskz=(np.arange(128)[None, :] >= np.arange(128)[:, None]).astype(ml_dtypes.bfloat16),
        sel8=np.kron(np.eye(8, dtype=np.float32), np.ones((1, 128), np.float32)),
    )
    wts = dict(
        ipw=np.ascontiguousarray(inputs["L0_attn_in_proj_w"]),
        convw=np.ascontiguousarray(inputs["L0_attn_conv_w"]),
        convb=np.ascontiguousarray(inputs["L0_attn_conv_b"]),
        convnw=np.ascontiguousarray(inputs["L0_attn_conv_norm_w"]),
        alog=np.ascontiguousarray(inputs["L0_attn_A_log"]),
        dtw=np.ascontiguousarray(inputs["L0_attn_dt_w"]),
        dtb=np.ascontiguousarray(inputs["L0_attn_dt_b"]),
        normw=np.ascontiguousarray(inputs["L0_attn_norm_w"]),
        onw=np.ascontiguousarray(inputs["L0_attn_out_norm_w"]),
        opw=np.ascontiguousarray(inputs["L0_attn_out_proj_w"]),
        wr=np.ascontiguousarray(inputs["L0_ffn_Wr"]),
        wg=np.ascontiguousarray(inputs["L0_ffn_Wg"]),
        wu=np.ascontiguousarray(inputs["L0_ffn_Wu"]),
        wd=np.ascontiguousarray(inputs["L0_ffn_Wd"]),
    )
    nc = _get_nc()
    in_maps = []
    for c in range(8):
        m = dict(x=np.ascontiguousarray(xfull[c]))
        m.update(wts)
        m.update(consts)
        in_maps.append(m)
    res = run_bass_kernel_spmd(nc, in_maps, core_ids=list(range(8)))
    out = np.stack([np.asarray(r["out"]) for r in res.results], axis=0)
    return out.astype(np.float32)


# version 05 (final)
# speedup vs baseline: 1.0797x; 1.0797x over previous
import contextlib
import numpy as np
import ml_dtypes
import concourse.bass as bass
import concourse.mybir as mybir
from concourse.bass_utils import run_bass_kernel_spmd

F32 = mybir.dt.float32
BF16 = mybir.dt.bfloat16
AF = mybir.ActivationFunctionType
ALU = mybir.AluOpType
AXX = mybir.AxisListType.X

BIG = 3.0e7
D = 2048
DI = 4096
NH = 64
DH = 64
NST = 128
G = 8
GN = G * NST
CONV = DI + 2 * GN
IP = DI + NH + 2 * GN + DI
L = 4096
TMT = 512
CH = 128
NTILE = L // TMT
NCH = TMT // CH
KC = D // 128
NMT = 81
EXP = 8
FI = 1408
NIT = FI // 128
NEI = EXP * NIT
NIC = DI // 128
NCC = CONV // 128

ENGS = ("tensor", "vector", "scalar", "sync", "gpsimd")


class Sem:
    def __init__(self, handle):
        self.h = handle
        self.n = 0


class Prog:
    def __init__(self):
        self.streams = {e: [] for e in ENGS}

    def op(self, eng, meth, *args, inc=None, **kwargs):
        if inc is not None:
            sem, n = inc
            sem.n += n

            def fn(e, meth=meth, args=args, kwargs=kwargs, sem=sem, n=n):
                getattr(e, meth)(*args, **kwargs).then_inc(sem.h, n)
        else:
            def fn(e, meth=meth, args=args, kwargs=kwargs):
                getattr(e, meth)(*args, **kwargs)
        self.streams[eng].append(fn)

    def wait(self, eng, sem, val=None):
        v = sem.n if val is None else val
        if v <= 0:
            return
        self.streams[eng].append(lambda e, h=sem.h, v=v: e.wait_ge(h, v))

    def snap(self, *sems):
        return [(s, s.n) for s in sems]

    def wait_snap(self, eng, snap):
        for s, v in snap:
            self.wait(eng, s, v)


def build_program(dbg=False):
    nc = bass.Bass("TRN2", target_bir_lowering=False)
    p = Prog()
    DK = "ExternalOutput" if dbg else "Internal"
    heavy = (dbg is True)

    d_x = nc.dram_tensor("x", [L, D], F32, kind="ExternalInput").ap()
    d_ipw = nc.dram_tensor("ipw", [IP, D], F32, kind="ExternalInput").ap()
    d_convw = nc.dram_tensor("convw", [CONV, 1, 4], F32, kind="ExternalInput").ap()
    d_convb = nc.dram_tensor("convb", [CONV], F32, kind="ExternalInput").ap()
    d_convnw = nc.dram_tensor("convnw", [CONV], F32, kind="ExternalInput").ap()
    d_alog = nc.dram_tensor("alog", [NH], F32, kind="ExternalInput").ap()
    d_dtw = nc.dram_tensor("dtw", [NH, NH], F32, kind="ExternalInput").ap()
    d_dtb = nc.dram_tensor("dtb", [NH], F32, kind="ExternalInput").ap()
    d_normw = nc.dram_tensor("normw", [D], F32, kind="ExternalInput").ap()
    d_onw = nc.dram_tensor("onw", [DI], F32, kind="ExternalInput").ap()
    d_opw = nc.dram_tensor("opw", [D, DI], F32, kind="ExternalInput").ap()
    d_wr = nc.dram_tensor("wr", [D, EXP], F32, kind="ExternalInput").ap()
    d_wg = nc.dram_tensor("wg", [EXP, D, FI], F32, kind="ExternalInput").ap()
    d_wu = nc.dram_tensor("wu", [EXP, D, FI], F32, kind="ExternalInput").ap()
    d_wd = nc.dram_tensor("wd", [EXP, FI, D], F32, kind="ExternalInput").ap()
    d_identb = nc.dram_tensor("identb", [128, 128], BF16, kind="ExternalInput").ap()
    d_identf = nc.dram_tensor("identf", [128, 128], F32, kind="ExternalInput").ap()
    d_negbigi = nc.dram_tensor("negbigi", [128, 128], BF16, kind="ExternalInput").ap()
    d_lt1 = nc.dram_tensor("lt1", [128, 128], BF16, kind="ExternalInput").ap()
    d_maskz = nc.dram_tensor("maskz", [128, 128], BF16, kind="ExternalInput").ap()
    d_sel8 = nc.dram_tensor("sel8", [8, 8 * 128], F32, kind="ExternalInput").ap()
    d_out = nc.dram_tensor("out", [L, D], F32, kind="ExternalOutput").ap()
    if heavy:
        d_dbg = nc.dram_tensor("dbgv", [128, 64], F32, kind="ExternalOutput").ap()
        d_drs = nc.dram_tensor("dbgrs", [NTILE, NCH, 128, 4], F32, kind="ExternalOutput").ap()
        d_dln = nc.dram_tensor("dbgln", [128, 16], F32, kind="ExternalOutput").ap()
        d_dxn = nc.dram_tensor("dbgxn", [NTILE, 128, KC * 512], BF16, kind="ExternalOutput").ap()
        d_dyf = nc.dram_tensor("dbgyf", [NTILE, 128, NIC * 512], BF16, kind="ExternalOutput").ap()
        d_dcv = nc.dram_tensor("dbgcv", [NCH, 128, CONV], BF16, kind="ExternalOutput").ap()
        d_dh = nc.dram_tensor("dbgh", [NCH, 128, DI], F32, kind="ExternalOutput").ap()
        d_dlm = nc.dram_tensor("dbglm", [NCH, 128, 4096], BF16, kind="ExternalOutput").ap()
        d_dhb = nc.dram_tensor("dbghb", [128, NEI * 512], BF16, kind="ExternalOutput").ap()
        d_dcb = nc.dram_tensor("dbgcb", [8, 512], F32, kind="ExternalOutput").ap()
        d_ddk = nc.dram_tensor("dbgdk", [NCH, 128, 192], F32, kind="ExternalOutput").ap()
        d_dyt = nc.dram_tensor("dbgyt", [NCH, 128, DI], F32, kind="ExternalOutput").ap()

    s_wip = nc.dram_tensor("s_wip", [NMT, 128, KC * 128], BF16, kind=DK).ap()
    s_wop = nc.dram_tensor("s_wop", [NIC, 128, D], BF16, kind=DK).ap()
    s_wg = nc.dram_tensor("s_wg", [EXP, NIT, 128, KC * 128], BF16, kind=DK).ap()
    s_wu = nc.dram_tensor("s_wu", [EXP, NIT, 128, KC * 128], BF16, kind=DK).ap()
    s_wd = nc.dram_tensor("s_wd", [EXP, NIT, 128, D], BF16, kind=DK).ap()
    s_conv = nc.dram_tensor("s_conv", [NCC, 128, L], BF16, kind=DK).ap()
    s_gate = nc.dram_tensor("s_gate", [NIC, 128, TMT], BF16, kind=DK).ap()
    s_a2 = nc.dram_tensor("s_a2", [NTILE * NCH, 128, 128], F32, kind=DK).ap()
    s_z = nc.dram_tensor("s_z", [L, D], F32, kind=DK).ap()
    s_x1f = nc.dram_tensor("s_x1f", [L, D], F32, kind=DK).ap()
    s_x1b = nc.dram_tensor("s_x1b", [NTILE * NCH, 128, D], BF16, kind=DK).ap()

    ctx = contextlib.ExitStack()
    E = ctx.enter_context

    sW = Sem(E(nc.semaphore("sW")))
    sG = Sem(E(nc.semaphore("sG")))
    sS = Sem(E(nc.semaphore("sS")))
    sP = Sem(E(nc.semaphore("sP")))
    sV = Sem(E(nc.semaphore("sV")))
    sA = Sem(E(nc.semaphore("sA")))
    ALLSEMS = (sW, sG, sS, sP, sV, sA)

    def barrier(engs=ENGS):
        snap = p.snap(*ALLSEMS)
        for e in engs:
            p.wait_snap(e, snap)

    identb = E(nc.sbuf_tensor("sb_identb", [128, 128], BF16))
    identf = E(nc.sbuf_tensor("sb_identf", [128, 128], F32))
    negbigi = E(nc.sbuf_tensor("sb_negbigi", [128, 128], BF16))
    lt1 = E(nc.sbuf_tensor("sb_lt1", [128, 128], BF16))
    maskz = E(nc.sbuf_tensor("sb_maskz", [128, 128], BF16))
    sel8 = E(nc.sbuf_tensor("sb_sel8", [8, 8 * 128], F32))
    onesf_row = E(nc.sbuf_tensor("onesf_row", [1, 128], F32))
    onesb_col = E(nc.sbuf_tensor("onesb_col", [128, 1], BF16))
    wrb = E(nc.sbuf_tensor("wrb", [128, KC * 8], BF16))
    c128a = E(nc.sbuf_tensor("c128a", [128, 8], F32))
    snapb = E(nc.sbuf_tensor("snapb", [128, 16], F32)) if heavy else None
    spc = E(nc.sbuf_tensor("spc", [128, 1], F32))
    flv = E(nc.sbuf_tensor("flv", [128, 512], F32))
    fla = E(nc.sbuf_tensor("fla", [128, 128], F32))
    mctx = contextlib.ExitStack()
    ME = mctx.enter_context
    convw_c = ME(nc.sbuf_tensor("convw_c", [128, NCC * 4], F32))
    convb_c = ME(nc.sbuf_tensor("convb_c", [128, NCC], F32))
    convnw_c = ME(nc.sbuf_tensor("convnw_c", [128, NCC], F32))
    nw_c = ME(nc.sbuf_tensor("nw_c", [128, KC], F32))
    onw_c = ME(nc.sbuf_tensor("onw_c", [128, NIC], F32))
    neA_col = ME(nc.sbuf_tensor("neA_col", [NH, 1], F32))
    dtb_col = ME(nc.sbuf_tensor("dtb_col", [NH, 1], F32))
    dtwT = ME(nc.sbuf_tensor("dtwT", [NH, NH], F32))
    h_st = ME(nc.sbuf_tensor("h_st", [128, NH * DH], F32))
    hb_st = ME(nc.sbuf_tensor("hb_st", [128, NH * DH], BF16))
    stgP = [ME(nc.sbuf_tensor(f"stgP{i}", [2, 1024], F32)) for i in range(2)]
    stgN = [ME(nc.sbuf_tensor(f"stgN{i}", [2, 1024], F32)) for i in range(2)]
    wipbuf = [ME(nc.sbuf_tensor(f"wipbuf{i}", [128, 2048], BF16)) for i in range(2)]
    cdrow = ME(nc.sbuf_tensor("cdrow", [1, 64], F32))
    CDsb = ME(nc.sbuf_tensor("CDsb", [128, 64], F32))
    expacs = ME(nc.sbuf_tensor("expacs", [128, 64], BF16))
    exprc = ME(nc.sbuf_tensor("exprc", [128, 64], BF16))
    a2 = ME(nc.sbuf_tensor("a2", [128, 128], F32))
    rcfm = ME(nc.sbuf_tensor("rcfm", [64, 128], F32))
    rsr = ME(nc.sbuf_tensor("rsr", [1, 256], F32))
    conv_sb = ME(nc.sbuf_tensor("convsb", [128, NCC * 132], BF16))
    pwraw = ME(nc.sbuf_tensor("pwraw", [128, 2 * 512], F32))
    pwstg = ME(nc.sbuf_tensor("pwstg", [128, 2 * 512], BF16))

    pmm = [E(nc.psum_tensor(f"pmm{i}", [128, 512], F32)) for i in range(4)]
    pR = E(nc.psum_tensor("pR", [128, 1024], F32))
    pT1 = E(nc.psum_tensor("pT1", [128, 1024], BF16))
    pT2 = E(nc.psum_tensor("pT2", [128, 1024], BF16))

    for dst, src in ((identb, d_identb), (identf, d_identf), (negbigi, d_negbigi),
                     (lt1, d_lt1), (maskz, d_maskz), (sel8, d_sel8)):
        p.op("sync", "dma_start", out=dst[:], in_=src[:], inc=(sW, 16))
    p.op("vector", "memset", onesf_row[:], 1.0)
    p.op("vector", "memset", onesb_col[:], 1.0)
    p.op("vector", "memset", h_st[:], 0.0)
    p.op("vector", "memset", hb_st[:], 0.0)
    for i in range(2):
        p.op("vector", "memset", stgP[i][0:2, :], 1.0)
        p.op("vector", "memset", stgN[i][0:2, :], 1.0, inc=(sV, 1))
    cw = d_convw.squeeze(1).rearrange("(ci pp) j -> pp ci j", pp=128)
    for i in range(8):
        p.op("sync", "dma_start",
             out=convw_c[:, i * 24:(i + 1) * 24].rearrange("pp (ci j) -> pp ci j", j=4),
             in_=cw[:, i * 6:(i + 1) * 6, :], inc=(sW, 16))
    with nc.sbuf_tensor("smallv", [128, 640], F32) as smallv, \
         nc.sbuf_tensor("wr_f", [128, KC * 8], F32) as wr_f:
        p.op("sync", "dma_start", out=smallv[0:48, 0:128],
             in_=d_convb.rearrange("(a b) -> a b", b=128), inc=(sW, 16))
        p.op("sync", "dma_start", out=smallv[0:48, 128:256],
             in_=d_convnw.rearrange("(a b) -> a b", b=128), inc=(sW, 16))
        p.op("sync", "dma_start", out=smallv[0:16, 256:384],
             in_=d_normw.rearrange("(a b) -> a b", b=128), inc=(sW, 16))
        p.op("sync", "dma_start", out=smallv[0:32, 384:512],
             in_=d_onw.rearrange("(a b) -> a b", b=128), inc=(sW, 16))
        p.op("sync", "dma_start", out=smallv[0:1, 512:576],
             in_=d_alog.rearrange("(a b) -> a b", a=1), inc=(sW, 16))
        p.op("sync", "dma_start", out=smallv[1:2, 512:576],
             in_=d_dtb.rearrange("(a b) -> a b", a=1), inc=(sW, 16))
        p.op("sync", "dma_start", out=dtwT[:, :], in_=d_dtw[:, :], inc=(sW, 16))
        p.op("sync", "dma_start",
             out=wr_f[:].rearrange("pp (kc e) -> pp kc e", e=8),
             in_=d_wr.rearrange("(kc pp) e -> pp kc e", pp=128), inc=(sW, 16))
        p.wait("tensor", sW)
        p.op("tensor", "transpose", pR[:, 0:48], smallv[0:48, 0:128], identf[0:48, 0:48])
        p.op("tensor", "transpose", pR[:, 48:96], smallv[0:48, 128:256],
             identf[0:48, 0:48])
        p.op("tensor", "transpose", pR[:, 96:112], smallv[0:16, 256:384],
             identf[0:16, 0:16])
        p.op("tensor", "transpose", pR[:, 112:144], smallv[0:32, 384:512],
             identf[0:32, 0:32])
        p.op("tensor", "transpose", pR[0:64, 144:146], smallv[0:2, 512:576],
             identf[0:2, 0:2])
        p.op("tensor", "transpose", pR[0:64, 160:224], dtwT[0:64, 0:64],
             identf[0:64, 0:64], inc=(sP, 1))
        p.wait("vector", sP)
        p.wait("vector", sW)
        p.op("vector", "tensor_copy", convb_c[:], pR[:, 0:48])
        p.op("vector", "tensor_copy", convnw_c[:], pR[:, 48:96])
        p.op("vector", "tensor_copy", nw_c[:], pR[:, 96:112])
        p.op("vector", "tensor_copy", onw_c[:], pR[:, 112:144])
        p.op("vector", "tensor_copy", dtb_col[:], pR[0:64, 145:146])
        p.op("vector", "tensor_copy", wrb[:], wr_f[:])
        p.op("vector", "tensor_copy", c128a[0:64, 0:1], pR[0:64, 144:145], inc=(sV, 1))
        p.wait("scalar", sV)
        p.op("scalar", "activation", neA_col[:], c128a[0:64, 0:1], AF.Exp, inc=(sA, 1))
        p.wait("vector", sA)
        p.op("vector", "tensor_scalar", neA_col[:], neA_col[:], -1.0, None, ALU.mult)
        p.op("vector", "tensor_copy", dtwT[0:64, 0:64], pR[0:64, 160:224])
        p.op("vector", "memset", wr_f[:, 0:8], 0.0, inc=(sV, 1))
        p.op("vector", "memset",
             conv_sb[:].rearrange("pp (ci w) -> pp ci w", ci=NCC)[:, :, 0:3], 0.0,
             inc=(sV, 1))

        if heavy:
            p.op("gpsimd", "dma_start", out=d_dbg[:, 0:16], in_=nw_c[:], inc=(sS, 16))
            p.op("gpsimd", "dma_start", out=d_dbg[:, 16:48], in_=onw_c[:], inc=(sS, 16))
            p.op("gpsimd", "dma_start", out=d_dbg[:, 48:64],
                 in_=convb_c[:, 0:16], inc=(sS, 16))

    barrier()

    with nc.sbuf_tensor("praw", [128, 2048], F32) as raw, \
         nc.sbuf_tensor("pcast", [128, 2048], BF16) as cast, \
         nc.sbuf_tensor("pstage", [128, 2048], BF16) as stage:

        ld_done = []
        st_done = []
        ev_done = []

        def prep_iter(n_rows, load_in, do_tr, fold_col, store_out):
            it = len(ld_done)
            if it >= 1:
                p.wait("sync", sV, ev_done[it - 1][0])
            lo = raw[0:n_rows, 0:2048]
            if load_in.ndim == 3:
                lo = lo.rearrange("pp (a b) -> pp a b", a=load_in.shape[1])
            p.op("sync", "dma_start", out=lo, in_=load_in, inc=(sW, 16))
            ld_done.append(sW.n)
            p.wait("vector", sW, ld_done[it])
            if it >= 1:
                p.wait("vector", sS, st_done[it - 1])
            if not do_tr:
                p.op("vector", "tensor_copy", stage[0:n_rows, 0:2048],
                     raw[0:n_rows, 0:2048], inc=(sV, 1))
                ev_done.append((sV.n, sV.n))
            else:
                p.op("vector", "tensor_copy", cast[0:n_rows, 0:2048],
                     raw[0:n_rows, 0:2048], inc=(sV, 1))
                raw_free = sV.n
                if n_rows < 128:
                    p.op("vector", "memset", cast[n_rows:128, 0:2048], 0.0, inc=(sV, 1))
                cast_v = sV.n
                p.wait("tensor", sV, cast_v)
                if it >= 1:
                    p.wait("tensor", sV, ev_done[it - 1][1])
                for kc in range(16):
                    dst = pT1 if kc < 8 else pT2
                    p.op("tensor", "transpose", dst[:, (kc % 8) * 128:(kc % 8 + 1) * 128],
                         cast[0:128, kc * 128:(kc + 1) * 128], identb[:],
                         inc=(sP, 1) if kc % 8 == 7 else None)
                p.wait("vector", sP)
                for kc in range(16):
                    src = pT1 if kc < 8 else pT2
                    sl = (kc % 8) * 128
                    if fold_col is not None:
                        p.op("vector", "tensor_scalar", stage[:, kc * 128:(kc + 1) * 128],
                             src[:, sl:sl + 128], fold_col(kc), None, ALU.mult,
                             inc=(sV, 1) if kc == 15 else None)
                    else:
                        p.op("vector", "tensor_copy", stage[:, kc * 128:(kc + 1) * 128],
                             src[:, sl:sl + 128], inc=(sV, 1) if kc == 15 else None)
                ev_done.append((raw_free, sV.n))
            p.wait("gpsimd", sV, ev_done[it][1])
            p.op("gpsimd", "dma_start", out=store_out, in_=stage[:], inc=(sS, 16))
            st_done.append(sS.n)

        def ip_rows(mt):
            if mt < 32:
                return mt * 128, 128
            if mt < 40:
                return DI + NH + (mt - 32) * 128, 128
            if mt < 48:
                return DI + NH + GN + (mt - 40) * 128, 128
            if mt < 80:
                return DI + NH + 2 * GN + (mt - 48) * 128, 128
            return DI, 64

        for mt in range(NMT):
            r0, nr = ip_rows(mt)
            prep_iter(nr, d_ipw[r0:r0 + nr, :], True,
                      lambda kc: nw_c[:, kc:kc + 1], s_wip[mt])
        for ic in range(NIC):
            src = d_opw[:, ic * 128:(ic + 1) * 128].rearrange("(ob pp) i -> pp ob i", pp=128)
            prep_iter(128, src, True, lambda kc, ic=ic: onw_c[:, ic:ic + 1], s_wop[ic])
        for e in range(EXP):
            for it_ in range(NIT):
                src = d_wg[e, :, it_ * 128:(it_ + 1) * 128].rearrange(
                    "(kc pp) i -> pp kc i", pp=128)
                prep_iter(128, src, False, None, s_wg[e][it_])
                src = d_wu[e, :, it_ * 128:(it_ + 1) * 128].rearrange(
                    "(kc pp) i -> pp kc i", pp=128)
                prep_iter(128, src, False, None, s_wu[e][it_])
        for e in range(EXP):
            for ic in range(NIT):
                prep_iter(128, d_wd[e, ic * 128:(ic + 1) * 128, :], False, None,
                          s_wd[e][ic])

    barrier()

    ip_load_done = [None] * (NMT * NTILE)
    ip_use_done = [None] * (NMT * NTILE)
    mcarry = {}
    pw_pairs = []
    for e in range(EXP):
        for it_ in range(NIT):
            for hf in range(4):
                rs = slice(hf * 512, (hf + 1) * 512)
                pw_pairs.append((d_wg[e, rs, it_ * 128:(it_ + 1) * 128].rearrange(
                    "(kc pp) i -> pp kc i", pp=128), s_wg[e][it_][:, rs]))
                pw_pairs.append((d_wu[e, rs, it_ * 128:(it_ + 1) * 128].rearrange(
                    "(kc pp) i -> pp kc i", pp=128), s_wu[e][it_][:, rs]))
    for e in range(EXP):
        for ic in range(NIT):
            for hf in range(4):
                rs = slice(hf * 512, (hf + 1) * 512)
                pw_pairs.append((d_wd[e, ic * 128:(ic + 1) * 128, rs],
                                 s_wd[e][ic][:, rs]))
    pw_pairs.clear()
    pw_state = {"k": 0, "pending": None, "st": {}, "cast": {}}

    def pw_flush():
        if pw_state["pending"] is None:
            return
        ldv, store_out, k = pw_state["pending"]
        pw_state["pending"] = None
        sl = slice((k % 2) * 512, (k % 2) * 512 + 512)
        p.wait("scalar", sG, ldv)
        if k >= 2 and (k - 2) in pw_state["st"]:
            p.wait("scalar", sS, pw_state["st"][k - 2])
        p.op("scalar", "activation", pwstg[:, sl], pwraw[:, sl], AF.Copy,
             inc=(sA, 1))
        pw_state["cast"][k] = sA.n
        p.wait("sync", sA, pw_state["cast"][k])
        p.op("sync", "dma_start", out=store_out, in_=pwstg[:, sl], inc=(sS, 16))
        pw_state["st"][k] = sS.n

    def pw_step(n=1):
        for _ in range(n):
            pw_flush()
            if not pw_pairs:
                return
            load_in, store_out = pw_pairs.pop(0)
            k = pw_state["k"]
            sl = slice((k % 2) * 512, (k % 2) * 512 + 512)
            if k >= 2 and (k - 2) in pw_state["cast"]:
                p.wait("gpsimd", sA, pw_state["cast"][k - 2])
            lo = pwraw[0:128, sl]
            if load_in.ndim == 3:
                lo = lo.rearrange("pp (a b) -> pp a b", a=load_in.shape[1])
            p.op("gpsimd", "dma_start", out=lo, in_=load_in, inc=(sG, 16))
            pw_state["pending"] = (sG.n, store_out, k)
            pw_state["k"] = k + 1

    for t in range(NTILE):
        tile_ctx = contextlib.ExitStack()
        TE = tile_ctx.enter_context
        xn_fm = TE(nc.sbuf_tensor(f"xnfm{t}", [128, KC * 512], BF16))
        Yfm = TE(nc.sbuf_tensor(f"Yfm{t}", [128, NIC * 512], BF16))

        with nc.sbuf_tensor(f"xbuf{t}", [128, 2 * 2048], F32) as xbuf, \
             nc.sbuf_tensor(f"xtmb{t}", [128, 2048], BF16) as xtmb, \
             nc.sbuf_tensor(f"sqj{t}", [128, 2048], BF16) as sqj:
            xld = [None] * NCH
            if t > 0:
                p.wait("gpsimd", sV, mcarry["x1v"])
                p.wait("vector", sP, mcarry["xtr"])
                p.wait("scalar", sV, mcarry["x1v"])

            def load_x(tc):
                p.op("gpsimd", "dma_start",
                     out=xbuf[:, (tc % 2) * 2048:(tc % 2 + 1) * 2048],
                     in_=d_x[t * TMT + tc * 128: t * TMT + (tc + 1) * 128, :],
                     inc=(sG, 16))
                xld[tc] = sG.n

            load_x(0)
            load_x(1)
            for tc in range(NCH):
                xb = xbuf[:, (tc % 2) * 2048:(tc % 2 + 1) * 2048]
                p.wait("scalar", sG, xld[tc])
                if tc >= 1:
                    p.wait("scalar", sV)
                p.op("scalar", "activation", sqj[:], xb, AF.Square,
                     accum_out=c128a[:, 0:1])
                p.op("scalar", "activation", fla[:, 64:128], fla[:, 0:64], AF.Copy)
                p.op("scalar", "activation", fla[:, 64:128], fla[:, 0:64], AF.Copy, inc=(sA, 1))
                p.wait("vector", sA)
                p.op("vector", "tensor_scalar", c128a[:, 4:5], c128a[:, 0:1],
                     1.0 / D, 1e-6, ALU.mult, ALU.add)
                p.op("vector", "tensor_scalar", flv[:, 256:512], flv[:, 0:256], 1.0, None, ALU.mult)
                p.op("vector", "tensor_scalar", flv[:, 256:512], flv[:, 0:256], 1.0, None, ALU.mult, inc=(sV, 1))
                p.wait("scalar", sV)
                p.op("scalar", "activation", c128a[:, 6:7], c128a[:, 4:5], AF.Sqrt)
                p.op("scalar", "activation", fla[:, 64:128], fla[:, 0:64], AF.Copy)
                p.op("scalar", "activation", fla[:, 64:128], fla[:, 0:64], AF.Copy, inc=(sA, 1))
                p.wait("vector", sA)
                p.op("vector", "reciprocal", c128a[:, 7:8], c128a[:, 6:7])
                p.op("vector", "tensor_scalar", flv[:, 256:512], flv[:, 0:256], 1.0, None, ALU.mult)
                p.op("vector", "tensor_scalar", flv[:, 256:512], flv[:, 0:256], 1.0, None, ALU.mult)
                p.op("vector", "tensor_scalar", xtmb[:], xb, c128a[:, 7:8], None,
                     ALU.mult, inc=(sV, 1))

                if tc + 2 < NCH:
                    p.wait("gpsimd", sV)
                    load_x(tc + 2)
                p.wait("tensor", sV)
                for kc in range(16):
                    dst = pT1 if kc < 8 else pT2
                    p.op("tensor", "transpose", dst[:, (kc % 8) * 128:(kc % 8 + 1) * 128],
                         xtmb[:, kc * 128:(kc + 1) * 128], identb[:],
                         inc=(sP, 1) if kc % 8 == 7 else None)
                p.wait("vector", sP)
                for kc in range(16):
                    src = pT1 if kc < 8 else pT2
                    p.op("vector", "tensor_copy",
                         xn_fm[:, kc * 512 + tc * 128: kc * 512 + (tc + 1) * 128],
                         src[:, (kc % 8) * 128:(kc % 8 + 1) * 128],
                         inc=(sV, 1) if kc == 15 else None)
            xn_ready = sV.n
            mcarry["xtr"] = sP.n
        if heavy:
            p.wait("gpsimd", sV, xn_ready)
            p.op("gpsimd", "dma_start", out=d_dxn[t], in_=xn_fm[:], inc=(sS, 16))

        with nc.sbuf_tensor(f"afm{t}", [64, 512], F32) as Afm:
            dtctx = contextlib.ExitStack()
            convstage = dtctx.enter_context(
                nc.sbuf_tensor(f"cst{t}", [128, 2 * 512], BF16))
            sgstage = dtctx.enter_context(
                nc.sbuf_tensor(f"sgs{t}", [128, 2 * 512], BF16))
            Adt = dtctx.enter_context(nc.sbuf_tensor(f"adt{t}", [64, 512], F32))
            dt1 = dtctx.enter_context(nc.sbuf_tensor(f"dt1{t}", [64, 512], F32))
            dt2 = dtctx.enter_context(nc.sbuf_tensor(f"dt2{t}", [64, 512], F32))
            dt3 = dtctx.enter_context(nc.sbuf_tensor(f"dt3{t}", [64, 512], F32))

            evac_ref = [None] * NMT
            cs_store = []
            sg_store = []
            for mt in range(NMT):
                gi = t * NMT + mt
                if gi >= 2 and ip_use_done[gi - 2] is not None:
                    p.wait("sync", sP, ip_use_done[gi - 2])
                p.op("sync", "dma_start", out=wipbuf[gi % 2][:], in_=s_wip[mt],
                     inc=(sW, 16))
                ip_load_done[gi] = sW.n
                p.wait("tensor", sW, ip_load_done[gi])
                if mt == 0:
                    p.wait("tensor", sV, xn_ready)
                if mt >= 4 and evac_ref[mt - 4] is not None:
                    s_, v_ = evac_ref[mt - 4]
                    p.wait("tensor", s_, v_)
                for kc in range(16):
                    p.op("tensor", "matmul", pmm[mt % 4][:],
                         wipbuf[gi % 2][:, kc * 128:(kc + 1) * 128],
                         xn_fm[:, kc * 512:(kc + 1) * 512],
                         start=(kc == 0), stop=(kc == 15),
                         inc=(sP, 1) if kc == 15 else None)
                ip_use_done[gi] = sP.n
                mmv = sP.n
                pw_step(1)
                if mt < 48:
                    p.wait("vector", sP, mmv)
                    if mt >= 2:
                        p.wait("vector", sS, cs_store[mt - 2])
                    p.op("vector", "tensor_copy",
                         convstage[:, (mt % 2) * 512:(mt % 2 + 1) * 512],
                         pmm[mt % 4][:], inc=(sV, 1))
                    evac_ref[mt] = (sV, sV.n)
                    p.wait("gpsimd", sV)
                    p.op("gpsimd", "dma_start",
                         out=s_conv[mt, :, t * TMT: t * TMT + 512],
                         in_=convstage[:, (mt % 2) * 512:(mt % 2 + 1) * 512],
                         inc=(sS, 16))
                    cs_store.append(sS.n)
                elif mt < 80:
                    g_i = mt - 48
                    p.wait("scalar", sP, mmv)
                    if g_i >= 2:
                        p.wait("scalar", sS, sg_store[g_i - 2])
                    p.op("scalar", "activation",
                         sgstage[:, (g_i % 2) * 512:(g_i % 2 + 1) * 512],
                         pmm[mt % 4][:], AF.Silu, inc=(sA, 1))
                    evac_ref[mt] = (sA, sA.n)
                    p.wait("gpsimd", sA)
                    p.op("gpsimd", "dma_start", out=s_gate[g_i],
                         in_=sgstage[:, (g_i % 2) * 512:(g_i % 2 + 1) * 512],
                         inc=(sS, 16))
                    sg_store.append(sS.n)
                else:
                    p.wait("vector", sP, mmv)
                    p.op("vector", "tensor_copy", Adt[:], pmm[mt % 4][0:64, :])
                    p.op("vector", "tensor_scalar", flv[:, 256:512], flv[:, 0:256], 1.0, None, ALU.mult)
                    p.op("vector", "tensor_scalar", flv[:, 256:512], flv[:, 0:256], 1.0, None, ALU.mult, inc=(sV, 1))
                    evac_ref[mt] = (sV, sV.n)
            conv_ready = sS.n
            gate_ready = sS.n

            p.wait("tensor", sV, evac_ref[80][1])
            p.op("tensor", "matmul", pR[0:64, 0:512], dtwT[0:64, 0:64], Adt[:],
                 start=True, stop=True, inc=(sP, 1))
            p.wait("vector", sP)
            p.op("vector", "tensor_scalar", dt1[:], pR[0:64, 0:512],
                 dtb_col[0:64, 0:1], None, ALU.add)
            p.op("vector", "tensor_scalar", dt2[:], dt1[:], -1.0, None, ALU.mult)
            p.op("vector", "tensor_tensor", dt2[:], dt1[:], dt2[:], ALU.max)
            p.op("vector", "tensor_scalar", flv[:, 256:512], flv[:, 0:256], 1.0, None, ALU.mult)
            p.op("vector", "tensor_scalar", flv[:, 256:512], flv[:, 0:256], 1.0, None, ALU.mult, inc=(sV, 1))
            p.wait("scalar", sV)
            p.op("scalar", "activation", dt3[:], dt2[:], AF.Exp, scale=-1.0)
            p.op("scalar", "activation", dt2[:], dt3[:], AF.Ln, bias=1.0)
            p.op("scalar", "activation", fla[:, 64:128], fla[:, 0:64], AF.Copy)
            p.op("scalar", "activation", fla[:, 64:128], fla[:, 0:64], AF.Copy, inc=(sA, 1))
            p.wait("vector", sA)
            p.op("vector", "tensor_scalar", dt3[:], dt1[:], 0.0, None, ALU.max)
            p.op("vector", "tensor_tensor", dt1[:], dt3[:], dt2[:], ALU.add)
            p.op("vector", "tensor_scalar", Afm[:], dt1[:], neA_col[0:64, 0:1], None,
                 ALU.mult, inc=(sV, 1))

            dtctx.close()
            chctx = contextlib.ExitStack()
            CE = chctx.enter_context
            convbuf = CE(nc.sbuf_tensor(f"cvb{t}", [128, CONV], BF16))
            convscr = CE(nc.sbuf_tensor(f"cscr{t}", [128, CONV], BF16))
            macA = CE(nc.sbuf_tensor(f"mca{t}", [128, 128], F32))
            macB = CE(nc.sbuf_tensor(f"mcb{t}", [128, 128], F32))
            Xtm = CE(nc.sbuf_tensor(f"xtm{t}", [128, DI], BF16))
            Xdf = convscr[:, 0:DI]
            Btm = CE(nc.sbuf_tensor(f"btm{t}", [128, GN], BF16))
            Zm = CE(nc.sbuf_tensor(f"zm{t}", [128, 1024], BF16))
            Lsb = CE(nc.sbuf_tensor(f"lsb{t}", [128, 1024], BF16))
            Msb = CE(nc.sbuf_tensor(f"msb{t}", [128, 2 * 1024], BF16))
            Ytm = CE(nc.sbuf_tensor(f"ytm{t}", [128, DI], F32))
            scrY = CE(nc.sbuf_tensor(f"scy{t}", [128, 512], F32))
            scrY2 = CE(nc.sbuf_tensor(f"scy2{t}", [128, 512], F32))
            sgbuf = CE(nc.sbuf_tensor(f"sgb{t}", [128, DI], BF16))
            if True:

                a_prev = mcarry.get("a")
                y_done_prev = mcarry.get("y")
                conv_prev = mcarry.get("conv")
                for tc in range(NCH):
                    gc = t * NCH + tc
                    par = tc % 2
                    csb3 = conv_sb[:].rearrange("pp (ci w) -> pp ci w", ci=NCC)
                    if gc > 0:
                        p.op("vector", "tensor_copy", csb3[:, :, 0:3],
                             csb3[:, :, 128:131], inc=(sV, 1))
                    tailv = sV.n
                    p.wait("gpsimd", sS, conv_ready)
                    p.wait("gpsimd", sV, tailv)
                    if conv_prev is not None:
                        p.wait("gpsimd", sV, conv_prev)
                    p.op("gpsimd", "dma_start",
                         out=csb3[:, :, 3:131],
                         in_=s_conv[:, :, gc * 128:(gc + 1) * 128].transpose([1, 0, 2]),
                         inc=(sG, 16))
                    convld = sG.n
                    if a_prev is not None:
                        p.wait_snap("vector", a_prev)
                    p.op("vector", "tensor_tensor_scan", a2[0:64, :],
                         Afm[:, tc * 128:(tc + 1) * 128],
                         Afm[:, tc * 128:(tc + 1) * 128], 0.0, ALU.add, ALU.bypass,
                         inc=(sV, 1))
                    p.wait("vector", sV)
                    p.op("vector", "memset", spc[:], 0.0)
                    p.op("vector", "tensor_scalar", rcfm[:], a2[0:64, :],
                         a2[0:64, 127:128], -1.0, ALU.subtract, ALU.mult)
                    p.op("vector", "tensor_scalar", a2[64:128, :], a2[0:64, :], -1.0,
                         None, ALU.mult, inc=(sV, 1))
                    av = sV.n
                    p.wait("gpsimd", sV, av)
                    p.op("gpsimd", "dma_start", out=s_a2[gc], in_=a2[:], inc=(sS, 16))
                    a2st = sS.n
                    p.wait("tensor", sV, av)
                    p.op("tensor", "transpose", pR[:, 512:576], a2[0:64, :],
                         identf[0:64, 0:64])
                    p.op("tensor", "transpose", pR[:, 576:640], rcfm[:],
                         identf[0:64, 0:64], inc=(sP, 1))
                    trv = sP.n
                    p.wait("scalar", sP, trv)
                    p.op("scalar", "activation", expacs[:], pR[:, 512:576], AF.Exp)
                    p.op("scalar", "activation", exprc[:], pR[:, 576:640], AF.Exp)
                    p.op("scalar", "activation", c128a[0:64, 0:1], a2[0:64, 127:128],
                         AF.Exp)
                    p.op("scalar", "activation", fla[:, 64:128], fla[:, 0:64], AF.Copy)
                    p.op("scalar", "activation", fla[:, 64:128], fla[:, 0:64], AF.Copy, inc=(sA, 1))
                    p.wait("tensor", sA)
                    p.op("tensor", "transpose", pR[0:1, 640:704], c128a[0:64, 0:1],
                         identf[0:64, 0:64], inc=(sP, 1))
                    p.wait("vector", sP)
                    p.op("vector", "tensor_copy", cdrow[:], pR[0:1, 640:704])
                    p.op("vector", "tensor_scalar", flv[:, 256:512], flv[:, 0:256], 1.0, None, ALU.mult)
                    p.op("vector", "tensor_scalar", flv[:, 256:512], flv[:, 0:256], 1.0, None, ALU.mult, inc=(sV, 1))
                    p.wait("tensor", sV)
                    p.op("tensor", "matmul", pR[:, 704:768], onesf_row[:], cdrow[:],
                         start=True, stop=True, inc=(sP, 1))
                    p.wait("vector", sP)
                    p.op("vector", "tensor_copy", CDsb[:], pR[:, 704:768], inc=(sV, 1))
                    a_prev = [(sS, a2st), (sP, sP.n), (sA, sA.n)]

                    p.wait("vector", sG, convld)
                    if y_done_prev is not None:
                        p.wait("vector", sP, y_done_prev)
                    for ci in range(NCC):
                        base = ci * 132
                        p.op("vector", "tensor_scalar", macA[:],
                             conv_sb[:, base + 0: base + 128],
                             convw_c[:, ci * 4 + 0: ci * 4 + 1],
                             convb_c[:, ci:ci + 1], ALU.mult, ALU.add)
                        p.op("vector", "scalar_tensor_tensor", macB[:],
                             conv_sb[:, base + 1: base + 129],
                             convw_c[:, ci * 4 + 1: ci * 4 + 2], macA[:],
                             ALU.mult, ALU.add)
                        p.op("vector", "scalar_tensor_tensor", macA[:],
                             conv_sb[:, base + 2: base + 130],
                             convw_c[:, ci * 4 + 2: ci * 4 + 3], macB[:],
                             ALU.mult, ALU.add)
                        p.op("vector", "scalar_tensor_tensor",
                             convbuf[:, ci * 128:(ci + 1) * 128],
                             conv_sb[:, base + 3: base + 131],
                             convw_c[:, ci * 4 + 3: ci * 4 + 4], macA[:],
                             ALU.mult, ALU.add,
                             inc=(sV, 1) if ci == NCC - 1 else None)
                    convv = sV.n
                    conv_prev = convv
                    p.wait("scalar", sV, convv)
                    p.op("scalar", "activation", convscr[:], convbuf[:], AF.Square,
                         inc=(sA, 1))
                    p.wait("tensor", sA)
                    for ci in range(NCC):
                        p.op("tensor", "matmul", pR[0:1, 768:896], onesb_col[:],
                             convscr[:, ci * 128:(ci + 1) * 128],
                             start=(ci == 0), stop=(ci == NCC - 1),
                             inc=(sP, 1) if ci == NCC - 1 else None)
                    p.wait("vector", sP)
                    p.op("vector", "tensor_scalar", rsr[0:1, 0:128], pR[0:1, 768:896],
                         1.0 / CONV, 1e-6, ALU.mult, ALU.add)
                    p.op("vector", "tensor_scalar", flv[:, 256:512], flv[:, 0:256], 1.0, None, ALU.mult)
                    p.op("vector", "tensor_scalar", flv[:, 256:512], flv[:, 0:256], 1.0, None, ALU.mult, inc=(sV, 1))
                    p.wait("scalar", sV)
                    p.op("scalar", "activation", rsr[0:1, 128:256], rsr[0:1, 0:128],
                         AF.Sqrt)
                    p.op("scalar", "activation", fla[:, 64:128], fla[:, 0:64], AF.Copy)
                    p.op("scalar", "activation", fla[:, 64:128], fla[:, 0:64], AF.Copy, inc=(sA, 1))
                    p.wait("vector", sA)
                    p.op("vector", "reciprocal", rsr[0:1, 192:256].bitcast(F32) if False else rsr[0:1, 0:128], rsr[0:1, 128:256])
                    p.op("vector", "tensor_scalar", flv[:, 256:512], flv[:, 0:256], 1.0, None, ALU.mult)
                    p.op("vector", "tensor_scalar", flv[:, 256:512], flv[:, 0:256], 1.0, None, ALU.mult, inc=(sV, 1))
                    p.wait("tensor", sV)
                    p.op("tensor", "matmul", pR[:, 896:1024], onesf_row[:],
                         rsr[0:1, 0:128], start=True, stop=True, inc=(sP, 1))
                    p.wait("vector", sP)
                    for ci in range(NCC):
                        p.op("vector", "scalar_tensor_tensor",
                             convscr[:, ci * 128:(ci + 1) * 128],
                             convbuf[:, ci * 128:(ci + 1) * 128],
                             convnw_c[:, ci:ci + 1], pR[:, 896:1024],
                             ALU.mult, ALU.mult,
                             inc=(sV, 1) if ci == NCC - 1 else None)
                    p.wait("scalar", sV)
                    p.op("scalar", "activation", convbuf[:], convscr[:], AF.Silu,
                         inc=(sA, 1))
                    convoutv = sA.n

                    p.wait("tensor", sA, convoutv)
                    tr_ev = []
                    for b in range(5):
                        dst = pT1 if b % 2 == 0 else pT2
                        if b >= 2:
                            p.wait("tensor", sV, tr_ev[b - 2])
                        for j in range(8):
                            idx = b * 8 + j
                            p.op("tensor", "transpose", dst[:, j * 128:(j + 1) * 128],
                                 convbuf[:, idx * 128:(idx + 1) * 128], identb[:],
                                 inc=(sP, 1) if j == 7 else None)
                        p.wait("vector", sP)
                        for j in range(8):
                            idx = b * 8 + j
                            tgt = Xtm[:, idx * 128:(idx + 1) * 128] if idx < 32 else \
                                Btm[:, (idx - 32) * 128:(idx - 31) * 128]
                            p.op("vector", "tensor_copy", tgt,
                                 dst[:, j * 128:(j + 1) * 128],
                                 inc=(sV, 1) if j == 7 else None)
                        tr_ev.append(sV.n)
                    p.op("vector", "tensor_tensor",
                         Xdf.rearrange("pp (h dd) -> pp h dd", h=64),
                         Xtm[:].rearrange("pp (h dd) -> pp h dd", h=64),
                         exprc[:].unsqueeze(2).broadcast_to([128, 64, 64]),
                         ALU.mult, inc=(sV, 1))

                    zmv = {}
                    for wv in range(2):
                        p.wait("tensor", sA, convoutv)
                        if wv == 1:
                            p.wait("tensor", sV, zmv[0])
                        for g in range(wv * 4, wv * 4 + 4):
                            p.op("tensor", "matmul",
                                 pmm[3][:, (g % 4) * 128:(g % 4 + 1) * 128],
                                 convbuf[:, (32 + g) * 128:(33 + g) * 128],
                                 convbuf[:, (40 + g) * 128:(41 + g) * 128],
                                 start=True, stop=True,
                                 inc=(sP, 1) if g % 4 == 3 else None)
                        p.wait("vector", sP)
                        for g in range(wv * 4, wv * 4 + 4):
                            p.op("vector", "tensor_tensor", Zm[:, g * 128:(g + 1) * 128],
                                 pmm[3][:, (g % 4) * 128:(g % 4 + 1) * 128], maskz[:],
                                 ALU.mult, inc=(sV, 1) if g % 4 == 3 else None)
                        zmv[wv] = sV.n

                    stg_ld = [None] * 8
                    r_done = [None] * 8

                    def load_stg(gg):
                        buf = gg % 2
                        if gg >= 2:
                            p.wait("sync", sP, r_done[gg - 2])
                        else:
                            p.wait("sync", sP)
                        p.wait("sync", sS, a2st)
                        p.op("sync", "dma_start", out=stgP[buf][0:1, :],
                             in_=s_a2[gc][gg * 8:(gg + 1) * 8, :].rearrange(
                                 "pp f -> (pp f)").unsqueeze(0), inc=(sW, 16))
                        p.op("sync", "dma_start", out=stgN[buf][1:2, :],
                             in_=s_a2[gc][64 + gg * 8: 64 + (gg + 1) * 8, :].rearrange(
                                 "pp f -> (pp f)").unsqueeze(0), inc=(sW, 16))
                        stg_ld[gg] = sW.n

                    load_stg(0)
                    load_stg(1)

                    y_ev = None
                    for g in range(8):
                        buf = g % 2
                        p.wait("tensor", sW, stg_ld[g])
                        p.wait("tensor", sA)
                        for hh in range(8):
                            w0 = hh * 128
                            p.op("tensor", "matmul", pR[:, hh * 128:(hh + 1) * 128],
                                 stgN[buf][:, w0:w0 + 128], stgP[buf][:, w0:w0 + 128],
                                 start=True, stop=False)
                            p.op("tensor", "matmul", pR[:, hh * 128:(hh + 1) * 128],
                                 negbigi[:], lt1[:], start=False, stop=True,
                                 inc=(sP, 1) if hh == 7 else None)
                        r_done[g] = sP.n
                        if g + 2 < 8:
                            load_stg(g + 2)
                        p.wait("scalar", sP, r_done[g])
                        if g >= 1:
                            p.wait("scalar", sV)
                        p.op("scalar", "activation", Lsb[:, 0:1024], pR[:],
                             AF.Exp, inc=(sA, 1))
                        lv = sA.n
                        p.wait("vector", sA, lv)
                        p.wait("vector", sV, zmv[g // 4])
                        p.op("vector", "tensor_tensor",
                             Msb[:, (g % 2) * 1024:(g % 2 + 1) * 1024].rearrange(
                                 "pp (a b) -> pp a b", a=8),
                             Lsb[:, 0:1024].rearrange(
                                 "pp (a b) -> pp a b", a=8),
                             Zm[:, g * 128:(g + 1) * 128].unsqueeze(1).broadcast_to(
                                 [128, 8, 128]),
                             ALU.mult)
                        p.op("vector", "tensor_scalar", flv[:, 256:512], flv[:, 0:256], 1.0, None, ALU.mult)
                        p.op("vector", "tensor_scalar", flv[:, 256:512], flv[:, 0:256], 1.0, None, ALU.mult, inc=(sV, 1))
                        mvv = sV.n
                        p.wait("tensor", sV, mvv)
                        if y_ev is not None:
                            p.wait("tensor", sV, y_ev)
                        for hh in range(8):
                            h = g * 8 + hh
                            p.op("tensor", "matmul", pmm[0][:, hh * 64:(hh + 1) * 64],
                                 Msb[:, (g % 2) * 1024 + hh * 128:
                                     (g % 2) * 1024 + (hh + 1) * 128],
                                 Xtm[:, h * 64:(h + 1) * 64], start=True, stop=True)
                        p.op("tensor", "matmul", pmm[1][:],
                             convbuf[:, (40 + g) * 128:(41 + g) * 128],
                             hb_st[:, g * 512:(g + 1) * 512], start=True, stop=True)
                        p.op("tensor", "matmul", pmm[2][:],
                             Btm[:, g * 128:(g + 1) * 128],
                             Xdf[:, g * 512:(g + 1) * 512], start=True, stop=True,
                             inc=(sP, 1))
                        yv = sP.n
                        p.wait("vector", sP, yv)
                        p.op("vector", "tensor_tensor",
                             scrY[:].rearrange("pp (a b) -> pp a b", a=8),
                             pmm[1][:].rearrange("pp (a b) -> pp a b", a=8),
                             expacs[:, g * 8:(g + 1) * 8].unsqueeze(2).broadcast_to(
                                 [128, 8, 64]), ALU.mult)
                        p.op("vector", "tensor_tensor", Ytm[:, g * 512:(g + 1) * 512],
                             pmm[0][:], scrY[:], ALU.add)
                        p.op("vector", "tensor_tensor",
                             scrY2[:].rearrange("pp (a b) -> pp a b", a=8),
                             h_st[:, g * 512:(g + 1) * 512].rearrange(
                                 "pp (a b) -> pp a b", a=8),
                             CDsb[:, g * 8:(g + 1) * 8].unsqueeze(2).broadcast_to(
                                 [128, 8, 64]), ALU.mult)
                        p.op("vector", "tensor_tensor", h_st[:, g * 512:(g + 1) * 512],
                             pmm[2][:], scrY2[:], ALU.add)
                        p.op("vector", "tensor_copy", hb_st[:, g * 512:(g + 1) * 512],
                             h_st[:, g * 512:(g + 1) * 512], inc=(sV, 1))
                        y_ev = sV.n

                    p.wait("scalar", sV, y_ev)
                    p.op("scalar", "activation", Xdf, Ytm[:], AF.Square,
                         accum_out=c128a[:, 0:1])
                    p.op("scalar", "activation", fla[:, 64:128], fla[:, 0:64], AF.Copy)
                    p.op("scalar", "activation", fla[:, 64:128], fla[:, 0:64], AF.Copy, inc=(sA, 1))
                    p.wait("vector", sA)
                    p.op("vector", "tensor_scalar", c128a[:, 4:5], c128a[:, 0:1],
                         1.0 / DI, 1e-6, ALU.mult, ALU.add)
                    p.op("vector", "tensor_scalar", flv[:, 256:512], flv[:, 0:256], 1.0, None, ALU.mult)
                    p.op("vector", "tensor_scalar", flv[:, 256:512], flv[:, 0:256], 1.0, None, ALU.mult, inc=(sV, 1))
                    p.wait("scalar", sV)
                    p.op("scalar", "activation", c128a[:, 6:7], c128a[:, 4:5], AF.Sqrt)
                    p.op("scalar", "activation", fla[:, 64:128], fla[:, 0:64], AF.Copy)
                    p.op("scalar", "activation", fla[:, 64:128], fla[:, 0:64], AF.Copy, inc=(sA, 1))
                    p.wait("vector", sA)
                    p.op("vector", "reciprocal", c128a[:, 7:8], c128a[:, 6:7])
                    p.op("vector", "tensor_scalar", flv[:, 256:512], flv[:, 0:256], 1.0, None, ALU.mult)
                    p.op("vector", "tensor_scalar", flv[:, 256:512], flv[:, 0:256], 1.0, None, ALU.mult)
                    p.op("vector", "tensor_scalar", Xtm[:], Ytm[:], c128a[:, 7:8],
                         None, ALU.mult, inc=(sV, 1))
                    ynv = sV.n
                    p.wait("gpsimd", sS, gate_ready)
                    if tc >= 1:
                        p.wait("gpsimd", sV)
                    p.op("gpsimd", "dma_start",
                         out=sgbuf[:].rearrange("pp (ic w) -> pp ic w", ic=NIC),
                         in_=s_gate[:, :, tc * 128:(tc + 1) * 128].transpose([1, 0, 2]),
                         inc=(sG, 16))
                    sgv = sG.n
                    p.wait("tensor", sV, ynv)
                    ytr_ev = []
                    for b in range(4):
                        dst = pT1 if b % 2 == 0 else pT2
                        if b >= 2:
                            p.wait("tensor", sV, ytr_ev[b - 2])
                        for j in range(8):
                            ic = b * 8 + j
                            p.op("tensor", "transpose", dst[:, j * 128:(j + 1) * 128],
                                 Xtm[:, ic * 128:(ic + 1) * 128], identb[:],
                                 inc=(sP, 1) if j == 7 else None)
                        p.wait("vector", sP)
                        p.wait("vector", sG, sgv)
                        for j in range(8):
                            ic = b * 8 + j
                            p.op("vector", "tensor_tensor",
                                 Yfm[:, ic * 512 + tc * 128: ic * 512 + (tc + 1) * 128],
                                 dst[:, j * 128:(j + 1) * 128],
                                 sgbuf[:, ic * 128:(ic + 1) * 128], ALU.mult,
                                 inc=(sV, 1) if j == 7 else None)
                        ytr_ev.append(sV.n)
                    y_done_prev = sP.n
                    if heavy and t == 1:
                        p.wait("gpsimd", sA, convoutv)
                        p.op("gpsimd", "dma_start", out=d_dcv[tc], in_=convbuf[:],
                             inc=(sS, 16))
                        p.wait("gpsimd", sV, y_ev)
                        p.op("gpsimd", "dma_start", out=d_dyt[tc], in_=Ytm[:],
                             inc=(sS, 16))
                        p.op("gpsimd", "dma_start", out=d_dh[tc], in_=h_st[:],
                             inc=(sS, 16))
                        p.op("gpsimd", "dma_start", out=d_ddk[tc, :, 0:64],
                             in_=CDsb[:], inc=(sS, 16))
                        p.op("gpsimd", "dma_start",
                             out=d_ddk[tc, :, 64:128].bitcast(BF16)[:, 0:64],
                             in_=expacs[:], inc=(sS, 16))
                        p.op("gpsimd", "dma_start",
                             out=d_ddk[tc, :, 128:192].bitcast(BF16)[:, 0:64],
                             in_=exprc[:], inc=(sS, 16))
                        p.op("gpsimd", "dma_start", out=d_dlm[tc, :, 0:1024],
                             in_=Lsb[:, 0:1024], inc=(sS, 16))
                        p.op("gpsimd", "dma_start", out=d_dlm[tc, :, 1024:2048],
                             in_=Msb[:, 1024:2048], inc=(sS, 16))

            mcarry["a"] = a_prev
            mcarry["y"] = y_done_prev
            mcarry["conv"] = conv_prev
            if heavy:
                p.wait("gpsimd", sV)
                p.op("gpsimd", "dma_start", out=d_dyf[t], in_=Yfm[:], inc=(sS, 16))
            chctx.close()
            with nc.sbuf_tensor(f"wob{t}", [128, 4 * 512], BF16) as wopbuf, \
                 nc.sbuf_tensor(f"zst{t}", [128, 2 * 512], F32) as zstage:
                p.wait("sync", sV)
                p.wait("sync", sA)
                op_mm = {}
                z_st = []
                for oq in range(4):
                    for ic in range(NIC):
                        n_ = oq * NIC + ic
                        i4 = n_ % 4
                        if n_ >= 4:
                            p.wait("sync", sP, op_mm[n_ - 4])
                        else:
                            p.wait("sync", sP)
                        p.op("sync", "dma_start",
                             out=wopbuf[:, i4 * 512:(i4 + 1) * 512],
                             in_=s_wop[ic][:, oq * 512:(oq + 1) * 512], inc=(sW, 16))
                        ldv = sW.n
                        p.wait("tensor", sW, ldv)
                        if n_ == 0:
                            p.wait("tensor", sV)
                        if ic == 0 and oq > 0:
                            p.wait("tensor", sV, z_ev)
                        for ts_ in range(4):
                            p.op("tensor", "matmul", pmm[ts_][:],
                                 Yfm[:, ic * 512 + ts_ * 128: ic * 512 + (ts_ + 1) * 128],
                                 wopbuf[:, i4 * 512:(i4 + 1) * 512],
                                 start=(ic == 0), stop=(ic == NIC - 1),
                                 inc=(sP, 1) if ts_ == 3 else None)
                        op_mm[n_] = sP.n
                    p.wait("vector", sP, op_mm[oq * NIC + NIC - 1])
                    for ts_ in range(4):
                        if len(z_st) >= 2:
                            p.wait("vector", sS, z_st[-2])
                        p.op("vector", "tensor_copy",
                             zstage[:, (ts_ % 2) * 512:(ts_ % 2 + 1) * 512],
                             pmm[ts_][:], inc=(sV, 1))
                        p.wait("gpsimd", sV)
                        p.op("gpsimd", "dma_start",
                             out=s_z[t * TMT + ts_ * 128: t * TMT + (ts_ + 1) * 128, oq * 512:(oq + 1) * 512],
                             in_=zstage[:, (ts_ % 2) * 512:(ts_ % 2 + 1) * 512],
                             inc=(sS, 16))
                        z_st.append(sS.n)
                    z_ev = sV.n
                z_ready = sS.n

        with nc.sbuf_tensor(f"zrd{t}", [128, 2 * 2048], F32) as zrd, \
             nc.sbuf_tensor(f"xr2{t}", [128, 2 * 2048], F32) as xrd, \
             nc.sbuf_tensor(f"zf{t}", [128, 2048], F32) as zf, \
             nc.sbuf_tensor(f"xtm2{t}", [128, 2048], BF16) as xtmb2, \
             nc.sbuf_tensor(f"x1s{t}", [128, 2048], BF16) as x1stage:
            x1f_st = [0] * 4
            x1b_st = [0] * 4
            tr_done = [0] * 4
            for ts_ in range(4):
                par = ts_ % 2
                p.wait("gpsimd", sS, z_ready)
                if ts_ >= 2:
                    p.wait("gpsimd", sV)
                    p.wait("gpsimd", sS, x1f_st[ts_ - 2])
                p.op("gpsimd", "dma_start", out=zrd[:, par * 2048:(par + 1) * 2048],
                     in_=s_z[t * TMT + ts_ * 128: t * TMT + (ts_ + 1) * 128, :], inc=(sG, 16))
                p.op("gpsimd", "dma_start", out=xrd[:, par * 2048:(par + 1) * 2048],
                     in_=d_x[t * TMT + ts_ * 128: t * TMT + (ts_ + 1) * 128, :],
                     inc=(sG, 16))
                ldv = sG.n
                p.wait("vector", sG, ldv)
                if ts_ >= 1:
                    p.wait("vector", sA)
                    p.wait("vector", sS, x1f_st[ts_ - 1])
                p.op("vector", "scalar_tensor_tensor", zf[:],
                     xrd[:, par * 2048:(par + 1) * 2048], 2.0,
                     zrd[:, par * 2048:(par + 1) * 2048], ALU.mult, ALU.add)
                if heavy and t == 99 and ts_ == 0:
                    p.op("vector", "tensor_reduce", snapb[:, 12:13],
                         zrd[:, par * 2048:(par + 1) * 2048], AXX, ALU.add)
                    p.op("vector", "tensor_reduce", snapb[:, 13:14],
                         xrd[:, par * 2048:(par + 1) * 2048], AXX, ALU.add)
                p.op("vector", "tensor_reduce", c128a[:, 0:1], zf[:], AXX, ALU.add,
                     inc=(sV, 1))
                p.wait("scalar", sV)
                if ts_ >= 1:
                    p.wait("scalar", sP, tr_done[ts_ - 1])
                p.op("scalar", "activation", xtmb2[:], zf[:], AF.Square)
                p.op("scalar", "activation", fla[:, 64:128], fla[:, 0:64], AF.Copy)
                p.op("scalar", "activation", fla[:, 64:128], fla[:, 0:64], AF.Copy, inc=(sA, 1))
                p.wait("vector", sA)
                p.op("vector", "tensor_reduce", c128a[:, 1:2], xtmb2[:], AXX, ALU.add)
                p.op("vector", "tensor_scalar", flv[:, 256:512], flv[:, 0:256], 1.0, None, ALU.mult)
                p.op("vector", "tensor_scalar", flv[:, 256:512], flv[:, 0:256], 1.0, None, ALU.mult)
                p.op("vector", "tensor_scalar", c128a[:, 2:3], c128a[:, 0:1], 1.0 / D,
                     None, ALU.mult)
                p.op("vector", "tensor_scalar", c128a[:, 4:5], c128a[:, 1:2], 1.0 / D,
                     1e-5, ALU.mult, ALU.add)
                p.op("vector", "tensor_scalar", flv[:, 256:512], flv[:, 0:256], 1.0, None, ALU.mult)
                p.op("vector", "tensor_scalar", flv[:, 256:512], flv[:, 0:256], 1.0, None, ALU.mult)
                p.op("vector", "tensor_tensor", c128a[:, 3:4], c128a[:, 2:3],
                     c128a[:, 2:3], ALU.mult)
                p.op("vector", "tensor_scalar", flv[:, 256:512], flv[:, 0:256], 1.0, None, ALU.mult)
                p.op("vector", "tensor_scalar", flv[:, 256:512], flv[:, 0:256], 1.0, None, ALU.mult)
                p.op("vector", "tensor_tensor", c128a[:, 5:6], c128a[:, 4:5],
                     c128a[:, 3:4], ALU.subtract)
                p.op("vector", "tensor_scalar", flv[:, 256:512], flv[:, 0:256], 1.0, None, ALU.mult)
                p.op("vector", "tensor_scalar", flv[:, 256:512], flv[:, 0:256], 1.0, None, ALU.mult, inc=(sV, 1))
                p.wait("scalar", sV)
                p.op("scalar", "activation", c128a[:, 6:7], c128a[:, 5:6], AF.Sqrt)
                p.op("scalar", "activation", fla[:, 64:128], fla[:, 0:64], AF.Copy)
                p.op("scalar", "activation", fla[:, 64:128], fla[:, 0:64], AF.Copy, inc=(sA, 1))
                p.wait("vector", sA)
                p.op("vector", "reciprocal", c128a[:, 7:8], c128a[:, 6:7],
                     inc=(sV, 1))
                p.op("vector", "tensor_scalar", xrd[:, par * 2048:(par + 1) * 2048],
                     zf[:], c128a[:, 2:3], None, ALU.subtract)
                p.op("vector", "tensor_scalar", zf[:],
                     xrd[:, par * 2048:(par + 1) * 2048], c128a[:, 7:8], None,
                     ALU.mult, inc=(sV, 1))
                x1v = sV.n
                p.wait("gpsimd", sV, x1v)
                p.op("gpsimd", "dma_start",
                     out=s_x1f[t * TMT + ts_ * 128: t * TMT + (ts_ + 1) * 128, :],
                     in_=zf[:], inc=(sS, 16))
                x1f_st[ts_] = sS.n
                p.op("vector", "tensor_copy", xtmb2[:], zf[:], inc=(sV, 1))
                p.wait("tensor", sV)
                for kc in range(16):
                    dst = pT1 if kc < 8 else pT2
                    p.op("tensor", "transpose", dst[:, (kc % 8) * 128:(kc % 8 + 1) * 128],
                         xtmb2[:, kc * 128:(kc + 1) * 128], identb[:],
                         inc=(sP, 1) if kc % 8 == 7 else None)
                tr_done[ts_] = sP.n
                p.wait("vector", sP)
                if ts_ >= 1:
                    p.wait("vector", sS, x1b_st[ts_ - 1])
                for kc in range(16):
                    src = pT1 if kc < 8 else pT2
                    p.op("vector", "tensor_copy", x1stage[:, kc * 128:(kc + 1) * 128],
                         src[:, (kc % 8) * 128:(kc % 8 + 1) * 128],
                         inc=(sV, 1) if kc == 15 else None)
                p.wait("gpsimd", sV)
                p.op("gpsimd", "dma_start", out=s_x1b[t * NCH + ts_],
                     in_=x1stage[:], inc=(sS, 16))
                x1b_st[ts_] = sS.n
        mcarry["x1v"] = x1v
        tile_ctx.close()

    pw_step(10 ** 9)
    barrier()
    mctx.close()

    fctx = contextlib.ExitStack()
    FE = fctx.enter_context
    x1fm = FE(nc.sbuf_tensor("x1fm", [128, KC * 512], BF16))
    hbuf = FE(nc.sbuf_tensor("hbuf", [128, NEI * 512], BF16))
    gbuf = FE(nc.sbuf_tensor("gbuf", [128, 2 * 2048], BF16))
    ubuf = FE(nc.sbuf_tensor("ubuf", [128, 2 * 2048], BF16))
    wdbuf = FE(nc.sbuf_tensor("wdb", [128, 4 * 512], BF16))
    lgt = FE(nc.sbuf_tensor("lgt", [8, 512], F32))
    lgtm = FE(nc.sbuf_tensor("lgtm", [128, 32], F32))
    etile = FE(nc.sbuf_tensor("etile", [128, 8], F32))
    eqt = FE(nc.sbuf_tensor("eqt", [128, 8], F32))
    e2t = FE(nc.sbuf_tensor("e2t", [128, 8], F32))
    combtm = FE(nc.sbuf_tensor("combtm", [128, 32], F32))
    combfm = FE(nc.sbuf_tensor("combfm", [8, 512], F32))
    wsb = FE(nc.sbuf_tensor("wsb", [128, 512], F32))
    sgb2 = FE(nc.sbuf_tensor("fsgb", [128, 512], BF16))
    tmpf = FE(nc.sbuf_tensor("tmpf", [128, 512], F32))
    z2b = FE(nc.sbuf_tensor("z2b", [128, 4 * 2048], F32))
    x1rd = FE(nc.sbuf_tensor("x1r", [128, 2 * 2048], F32))
    zf2 = FE(nc.sbuf_tensor("fzf", [128, 2048], F32))
    outstage = FE(nc.sbuf_tensor("outs", [128, 2048], F32))
    if True:

        x1ld = {}
        rcombv = {}
        gu_last2 = None
        z2v_prev = None
        down_hist = []

        def emit_x1_loads(t, wait_sp):
            if wait_sp is not None:
                p.wait("gpsimd", sP, wait_sp)
            for kc in range(KC):
                p.op("gpsimd", "dma_start",
                     out=x1fm[:, kc * 512:(kc + 1) * 512].rearrange(
                         "pp (tc w) -> pp tc w", tc=NCH),
                     in_=s_x1b[t * NCH:(t + 1) * NCH, :, kc * 128:(kc + 1) * 128]
                     .transpose([1, 0, 2]), inc=(sG, 16))
            x1ld[t] = sG.n

        def router_head(t):
            p.wait("tensor", sG, x1ld[t])
            if t > 0:
                p.wait("tensor", sV, rcombv[t - 1])
            for kc in range(KC):
                p.op("tensor", "matmul", pR[0:8, 0:512], wrb[:, kc * 8:(kc + 1) * 8],
                     x1fm[:, kc * 512:(kc + 1) * 512], start=(kc == 0),
                     stop=(kc == 15), inc=(sP, 1) if kc == 15 else None)
            p.wait("vector", sP)
            p.op("vector", "tensor_copy", lgt[:], pR[0:8, 0:512])
            p.op("vector", "tensor_scalar", flv[:, 256:512], flv[:, 0:256], 1.0, None, ALU.mult)
            p.op("vector", "tensor_scalar", flv[:, 256:512], flv[:, 0:256], 1.0, None, ALU.mult, inc=(sV, 1))
            p.wait("tensor", sV)
            for ts_ in range(4):
                p.op("tensor", "transpose", pR[:, 512 + ts_ * 8: 512 + (ts_ + 1) * 8],
                     lgt[0:8, ts_ * 128:(ts_ + 1) * 128], identf[0:8, 0:8],
                     inc=(sP, 1) if ts_ == 3 else None)
            p.wait("vector", sP)
            p.op("vector", "tensor_copy", lgtm[:], pR[:, 512:544])
            p.op("vector", "tensor_scalar", flv[:, 256:512], flv[:, 0:256], 1.0, None, ALU.mult)
            p.op("vector", "tensor_scalar", flv[:, 256:512], flv[:, 0:256], 1.0, None, ALU.mult, inc=(sV, 1))

        def router_iter(t, ts_):
            sl = lgtm[:, ts_ * 8:(ts_ + 1) * 8]
            p.op("vector", "reduce_max", c128a[:, 0:1], sl, AXX)
            p.op("vector", "tensor_scalar", flv[:, 256:512], flv[:, 0:256], 1.0, None, ALU.mult)
            p.op("vector", "tensor_scalar", flv[:, 256:512], flv[:, 0:256], 1.0, None, ALU.mult)
            p.op("vector", "tensor_scalar", c128a[:, 2:3], c128a[:, 0:1], -1.0,
                 None, ALU.mult)
            p.op("vector", "tensor_scalar", flv[:, 256:512], flv[:, 0:256], 1.0, None, ALU.mult)
            p.op("vector", "tensor_scalar", flv[:, 256:512], flv[:, 0:256], 1.0, None, ALU.mult, inc=(sV, 1))
            p.wait("scalar", sV)
            p.op("scalar", "activation", etile[:], sl, AF.Exp,
                 bias=c128a[:, 2:3])
            p.op("scalar", "activation", fla[:, 64:128], fla[:, 0:64], AF.Copy)
            p.op("scalar", "activation", fla[:, 64:128], fla[:, 0:64], AF.Copy, inc=(sA, 1))
            p.wait("vector", sA)
            p.op("vector", "reduce_max", c128a[:, 1:2], etile[:], AXX)
            p.op("vector", "tensor_scalar", flv[:, 256:512], flv[:, 0:256], 1.0, None, ALU.mult)
            p.op("vector", "tensor_scalar", flv[:, 256:512], flv[:, 0:256], 1.0, None, ALU.mult)
            p.op("vector", "tensor_tensor", eqt[:], etile[:],
                 c128a[:, 1:2].broadcast_to([128, 8]), ALU.is_ge)
            p.op("vector", "tensor_scalar", flv[:, 256:512], flv[:, 0:256], 1.0, None, ALU.mult)
            p.op("vector", "tensor_scalar", flv[:, 256:512], flv[:, 0:256], 1.0, None, ALU.mult)
            p.op("vector", "scalar_tensor_tensor", e2t[:], eqt[:], -BIG, etile[:],
                 ALU.mult, ALU.add)
            p.op("vector", "tensor_scalar", flv[:, 256:512], flv[:, 0:256], 1.0, None, ALU.mult)
            p.op("vector", "tensor_scalar", flv[:, 256:512], flv[:, 0:256], 1.0, None, ALU.mult)
            p.op("vector", "reduce_max", c128a[:, 4:5], e2t[:], AXX)
            p.op("vector", "tensor_scalar", flv[:, 256:512], flv[:, 0:256], 1.0, None, ALU.mult)
            p.op("vector", "tensor_scalar", flv[:, 256:512], flv[:, 0:256], 1.0, None, ALU.mult)
            p.op("vector", "tensor_tensor", eqt[:], etile[:],
                 c128a[:, 4:5].broadcast_to([128, 8]), ALU.is_ge)
            p.op("vector", "tensor_scalar", flv[:, 256:512], flv[:, 0:256], 1.0, None, ALU.mult)
            p.op("vector", "tensor_scalar", flv[:, 256:512], flv[:, 0:256], 1.0, None, ALU.mult)
            p.op("vector", "tensor_tensor", c128a[:, 5:6], c128a[:, 1:2],
                 c128a[:, 4:5], ALU.add)
            p.op("vector", "tensor_scalar", flv[:, 256:512], flv[:, 0:256], 1.0, None, ALU.mult)
            p.op("vector", "tensor_scalar", flv[:, 256:512], flv[:, 0:256], 1.0, None, ALU.mult)
            p.op("vector", "reciprocal", c128a[:, 6:7], c128a[:, 5:6])
            p.op("vector", "tensor_scalar", flv[:, 256:512], flv[:, 0:256], 1.0, None, ALU.mult)
            p.op("vector", "tensor_scalar", flv[:, 256:512], flv[:, 0:256], 1.0, None, ALU.mult)
            p.op("vector", "tensor_tensor", e2t[:], etile[:], eqt[:], ALU.mult)
            p.op("vector", "tensor_scalar", flv[:, 256:512], flv[:, 0:256], 1.0, None, ALU.mult)
            p.op("vector", "tensor_scalar", flv[:, 256:512], flv[:, 0:256], 1.0, None, ALU.mult)
            p.op("vector", "tensor_tensor", combtm[:, ts_ * 8:(ts_ + 1) * 8],
                 e2t[:], c128a[:, 6:7].broadcast_to([128, 8]), ALU.mult)
            p.op("vector", "tensor_scalar", flv[:, 256:512], flv[:, 0:256], 1.0, None, ALU.mult)
            p.op("vector", "tensor_scalar", flv[:, 256:512], flv[:, 0:256], 1.0, None, ALU.mult, inc=(sV, 1))

        def router_tail(t):
            p.wait("tensor", sV)
            for ts_ in range(4):
                p.op("tensor", "transpose", pR[0:8, ts_ * 128:(ts_ + 1) * 128],
                     combtm[:, ts_ * 8:(ts_ + 1) * 8], identf[:],
                     inc=(sP, 1) if ts_ == 3 else None)
            p.wait("vector", sP)
            p.op("vector", "tensor_copy", combfm[:], pR[0:8, 0:512])
            p.op("vector", "tensor_scalar", flv[:, 256:512], flv[:, 0:256], 1.0, None, ALU.mult)
            p.op("vector", "tensor_scalar", flv[:, 256:512], flv[:, 0:256], 1.0, None, ALU.mult, inc=(sV, 1))
            rcombv[t] = sV.n

        emit_x1_loads(0, None)
        router_head(0)
        for _ts in range(4):
            router_iter(0, _ts)
        router_tail(0)

        for t in range(NTILE):
            combv = rcombv[t]
            gu_mm = {}
            glue_done = {}
            for e in range(EXP):
                p.wait("tensor", sV, combv)
                if e == 0:
                    if z2v_prev is not None:
                        p.wait("tensor", sV, z2v_prev)
                else:
                    p.wait("tensor", sV, glue_done[e * NIT - 1])
                p.op("tensor", "matmul", pmm[3][:], sel8[:, e * 128:(e + 1) * 128],
                     combfm[:], start=True, stop=True, inc=(sP, 1))
                p.wait("vector", sP)
                p.op("vector", "tensor_copy", wsb[:], pmm[3][:], inc=(sV, 1))
                wsbv = sV.n
                for it_ in range(NIT):
                    ei = e * NIT + it_
                    par = ei % 2
                    if ei >= 2:
                        p.wait("sync", sP, gu_mm[ei - 2])
                    elif gu_last2 is not None:
                        p.wait("sync", sP, gu_last2)
                    p.op("sync", "dma_start", out=gbuf[:, par * 2048:(par + 1) * 2048],
                         in_=s_wg[e][it_], inc=(sW, 16))
                    p.op("sync", "dma_start", out=ubuf[:, par * 2048:(par + 1) * 2048],
                         in_=s_wu[e][it_], inc=(sW, 16))
                    ldv = sW.n
                    p.wait("tensor", sW, ldv)
                    if ei >= 2:
                        p.wait("tensor", sV, glue_done[ei - 2])
                    if it_ == 1:
                        p.wait("tensor", sV, wsbv)
                    pg = pmm[0] if ei % 2 == 0 else pmm[2]
                    pu = pmm[1] if ei % 2 == 0 else pmm[3]
                    for kc in range(KC):
                        p.op("tensor", "matmul", pg[:],
                             gbuf[:, par * 2048 + kc * 128: par * 2048 + (kc + 1) * 128],
                             x1fm[:, kc * 512:(kc + 1) * 512], start=(kc == 0),
                             stop=(kc == 15))
                    for kc in range(KC):
                        p.op("tensor", "matmul", pu[:],
                             ubuf[:, par * 2048 + kc * 128: par * 2048 + (kc + 1) * 128],
                             x1fm[:, kc * 512:(kc + 1) * 512], start=(kc == 0),
                             stop=(kc == 15), inc=(sP, 1) if kc == 15 else None)
                    gu_mm[ei] = sP.n
                    p.wait("scalar", sP, gu_mm[ei])
                    p.op("scalar", "activation", sgb2[:], pg[:], AF.Silu)
                    p.op("scalar", "activation", fla[:, 64:128], fla[:, 0:64], AF.Copy)
                    p.op("scalar", "activation", fla[:, 64:128], fla[:, 0:64], AF.Copy, inc=(sA, 1))
                    p.wait("vector", sA)
                    p.op("vector", "tensor_tensor", tmpf[:], sgb2[:], pu[:], ALU.mult)
                    p.wait("vector", sV, wsbv)
                    p.op("vector", "tensor_tensor", hbuf[:, ei * 512:(ei + 1) * 512],
                         tmpf[:], wsb[:], ALU.mult, inc=(sV, 1))
                    glue_done[ei] = sV.n
            hv = sV.n
            gu_last2 = gu_mm[NEI - 1]

            if heavy and t == NTILE - 1:
                p.wait("gpsimd", sV, hv)
                p.op("gpsimd", "dma_start", out=d_dhb[:], in_=hbuf[:], inc=(sS, 16))
                p.op("gpsimd", "dma_start", out=d_dcb[:], in_=combfm[:], inc=(sS, 16))

            if t + 1 < NTILE:
                emit_x1_loads(t + 1, gu_mm[NEI - 1])

            for oq in range(4):
                for ei in range(NEI):
                    e, ic = divmod(ei, NIT)
                    i4 = ei % 4
                    if len(down_hist) >= 4:
                        p.wait("sync", sP, down_hist[-4])
                    p.op("sync", "dma_start", out=wdbuf[:, i4 * 512:(i4 + 1) * 512],
                         in_=s_wd[e][ic][:, oq * 512:(oq + 1) * 512], inc=(sW, 16))
                    ldv = sW.n
                    p.wait("tensor", sW, ldv)
                    if ei == 0:
                        p.wait("tensor", sV, hv)
                        if oq > 0:
                            p.wait("tensor", sV, z2_ev)
                    for ts_ in range(4):
                        p.op("tensor", "matmul", pmm[ts_][:],
                             hbuf[:, ei * 512 + ts_ * 128: ei * 512 + (ts_ + 1) * 128],
                             wdbuf[:, i4 * 512:(i4 + 1) * 512],
                             start=(ei == 0), stop=(ei == NEI - 1),
                             inc=(sP, 1) if ts_ == 3 else None)
                    down_hist.append(sP.n)
                p.wait("vector", sP, down_hist[-1])
                for ts_ in range(4):
                    p.op("vector", "tensor_copy",
                         z2b[:, ts_ * 2048 + oq * 512: ts_ * 2048 + (oq + 1) * 512],
                         pmm[ts_][:], inc=(sV, 1) if ts_ == 3 else None)
                z2_ev = sV.n
                if t + 1 < NTILE:
                    if oq == 0:
                        router_head(t + 1)
                    router_iter(t + 1, oq)
                    if oq == 3:
                        router_tail(t + 1)
            z2v = sV.n
            z2v_prev = z2v

            for ts_ in range(4):
                par = ts_ % 2
                if ts_ >= 2:
                    p.wait("gpsimd", sV)
                p.op("gpsimd", "dma_start", out=x1rd[:, par * 2048:(par + 1) * 2048],
                     in_=s_x1f[t * TMT + ts_ * 128: t * TMT + (ts_ + 1) * 128, :],
                     inc=(sG, 16))
                ldv = sG.n
                p.wait("vector", sG, ldv)
                p.wait("vector", sV, z2v)
                if ts_ >= 1:
                    p.wait("vector", sA)
                    p.wait("vector", sS, out_st_prev)
                p.op("vector", "tensor_tensor", zf2[:],
                     z2b[:, ts_ * 2048:(ts_ + 1) * 2048],
                     x1rd[:, par * 2048:(par + 1) * 2048], ALU.add)
                p.op("vector", "tensor_reduce", c128a[:, 0:1], zf2[:], AXX, ALU.add,
                     inc=(sV, 1))
                p.wait("scalar", sV)
                p.op("scalar", "activation", outstage.bitcast(BF16)[:, 0:2048], zf2[:],
                     AF.Square)
                p.op("scalar", "activation", fla[:, 64:128], fla[:, 0:64], AF.Copy)
                p.op("scalar", "activation", fla[:, 64:128], fla[:, 0:64], AF.Copy, inc=(sA, 1))
                p.wait("vector", sA)
                if ts_ >= 1:
                    p.wait("vector", sS, out_st_prev)
                p.op("vector", "tensor_reduce", c128a[:, 1:2],
                     outstage.bitcast(BF16)[:, 0:2048], AXX, ALU.add)
                p.op("vector", "tensor_scalar", flv[:, 256:512], flv[:, 0:256], 1.0, None, ALU.mult)
                p.op("vector", "tensor_scalar", flv[:, 256:512], flv[:, 0:256], 1.0, None, ALU.mult)
                p.op("vector", "tensor_scalar", c128a[:, 2:3], c128a[:, 0:1], 1.0 / D,
                     None, ALU.mult)
                p.op("vector", "tensor_scalar", c128a[:, 4:5], c128a[:, 1:2], 1.0 / D,
                     1e-5, ALU.mult, ALU.add)
                p.op("vector", "tensor_scalar", flv[:, 256:512], flv[:, 0:256], 1.0, None, ALU.mult)
                p.op("vector", "tensor_scalar", flv[:, 256:512], flv[:, 0:256], 1.0, None, ALU.mult)
                p.op("vector", "tensor_tensor", c128a[:, 3:4], c128a[:, 2:3],
                     c128a[:, 2:3], ALU.mult)
                p.op("vector", "tensor_scalar", flv[:, 256:512], flv[:, 0:256], 1.0, None, ALU.mult)
                p.op("vector", "tensor_scalar", flv[:, 256:512], flv[:, 0:256], 1.0, None, ALU.mult)
                p.op("vector", "tensor_tensor", c128a[:, 5:6], c128a[:, 4:5],
                     c128a[:, 3:4], ALU.subtract)
                p.op("vector", "tensor_scalar", flv[:, 256:512], flv[:, 0:256], 1.0, None, ALU.mult)
                p.op("vector", "tensor_scalar", flv[:, 256:512], flv[:, 0:256], 1.0, None, ALU.mult, inc=(sV, 1))
                p.wait("scalar", sV)
                p.op("scalar", "activation", c128a[:, 6:7], c128a[:, 5:6], AF.Sqrt)
                p.op("scalar", "activation", fla[:, 64:128], fla[:, 0:64], AF.Copy)
                p.op("scalar", "activation", fla[:, 64:128], fla[:, 0:64], AF.Copy, inc=(sA, 1))
                p.wait("vector", sA)
                p.op("vector", "reciprocal", c128a[:, 7:8], c128a[:, 6:7], inc=(sV, 1))
                p.op("vector", "tensor_scalar", outstage[:], zf2[:], c128a[:, 2:3],
                     None, ALU.subtract)
                p.op("vector", "tensor_scalar", zf2[:], outstage[:], c128a[:, 7:8],
                     None, ALU.mult, inc=(sV, 1))
                p.wait("gpsimd", sV)
                p.op("gpsimd", "dma_start",
                     out=d_out[t * TMT + ts_ * 128: t * TMT + (ts_ + 1) * 128, :],
                     in_=zf2[:], inc=(sS, 16))
                out_st_prev = sS.n

        barrier(("tensor", "vector", "scalar", "gpsimd"))

    fctx.close()
    if heavy:
        p.wait("gpsimd", sV)
        p.op("gpsimd", "dma_start", out=d_dln[:], in_=snapb[:], inc=(sS, 16))
    p.wait("gpsimd", sS)
    p.wait("gpsimd", sW)
    p.wait("gpsimd", sG)

    with nc.Block() as block:
        @block.tensor
        def _(eng):
            for fn in p.streams["tensor"]:
                fn(eng)

        @block.vector
        def _(eng):
            for fn in p.streams["vector"]:
                fn(eng)

        @block.scalar
        def _(eng):
            for fn in p.streams["scalar"]:
                fn(eng)

        @block.sync
        def _(eng):
            for fn in p.streams["sync"]:
                fn(eng)

        @block.gpsimd
        def _(eng):
            for fn in p.streams["gpsimd"]:
                fn(eng)

    ctx.close()
    return nc


_NC_CACHE = None


def _get_nc():
    global _NC_CACHE
    if _NC_CACHE is None:
        _NC_CACHE = build_program()
    return _NC_CACHE


def kernel(**inputs):
    xfull = np.ascontiguousarray(inputs["L0_attn_x"])
    consts = dict(
        identb=np.eye(128, dtype=ml_dtypes.bfloat16),
        identf=np.eye(128, dtype=np.float32),
        negbigi=(-BIG * np.eye(128)).astype(ml_dtypes.bfloat16),
        lt1=(np.arange(128)[None, :] < np.arange(128)[:, None]).astype(ml_dtypes.bfloat16),
        maskz=(np.arange(128)[None, :] >= np.arange(128)[:, None]).astype(ml_dtypes.bfloat16),
        sel8=np.kron(np.eye(8, dtype=np.float32), np.ones((1, 128), np.float32)),
    )
    wts = dict(
        ipw=np.ascontiguousarray(inputs["L0_attn_in_proj_w"]),
        convw=np.ascontiguousarray(inputs["L0_attn_conv_w"]),
        convb=np.ascontiguousarray(inputs["L0_attn_conv_b"]),
        convnw=np.ascontiguousarray(inputs["L0_attn_conv_norm_w"]),
        alog=np.ascontiguousarray(inputs["L0_attn_A_log"]),
        dtw=np.ascontiguousarray(inputs["L0_attn_dt_w"]),
        dtb=np.ascontiguousarray(inputs["L0_attn_dt_b"]),
        normw=np.ascontiguousarray(inputs["L0_attn_norm_w"]),
        onw=np.ascontiguousarray(inputs["L0_attn_out_norm_w"]),
        opw=np.ascontiguousarray(inputs["L0_attn_out_proj_w"]),
        wr=np.ascontiguousarray(inputs["L0_ffn_Wr"]),
        wg=np.ascontiguousarray(inputs["L0_ffn_Wg"]),
        wu=np.ascontiguousarray(inputs["L0_ffn_Wu"]),
        wd=np.ascontiguousarray(inputs["L0_ffn_Wd"]),
    )
    nc = _get_nc()
    in_maps = []
    for c in range(8):
        m = dict(x=np.ascontiguousarray(xfull[c]))
        m.update(wts)
        m.update(consts)
        in_maps.append(m)
    res = run_bass_kernel_spmd(nc, in_maps, core_ids=list(range(8)))
    out = np.stack([np.asarray(r["out"]) for r in res.results], axis=0)
    return out.astype(np.float32)
